# version 02
# baseline (speedup 1.0000x reference)
import contextlib


ENGS = ("tensor", "vector", "scalar", "gpsimd", "sync")


class Buf:
    __slots__ = ("w", "r", "ctr")

    def __init__(self):
        self.w = None
        self.r = []
        self.ctr = None

    def write_wait(self):
        return self.r + ([self.w] if self.w else [])

    def read_wait(self):
        return [self.w] if self.w else []


def step(p, eng, fn, reads=(), writes=(), inc=None, extra_wait=(), inc_k=1):
    wait = [t for t in extra_wait if t is not None]
    for b in reads:
        wait += b.read_wait()
    for b in writes:
        wait += b.write_wait()
    tok = p.do(eng, fn, wait=wait, inc=inc, inc_k=inc_k)
    for b in reads:
        b.r.append(tok)
    for b in writes:
        b.w = tok
        b.r = []
    return tok


class Ctr:
    __slots__ = ("name", "n", "sem")

    def __init__(self, name):
        self.name = name
        self.n = 0
        self.sem = None

    def tok(self):
        return (self, self.n)


class Prog:
    def __init__(self, nc):
        self.nc = nc
        self.steps = {e: [] for e in ENGS}
        self.ctrs = []

    def ctr(self, name):
        c = Ctr(name)
        self.ctrs.append(c)
        return c

    def do(self, eng, fn, wait=(), inc=None, inc_k=1):
        tok = None
        if inc is not None:
            inc.n += inc_k
            tok = (inc, inc.n)
        self.steps[eng].append((tuple(wait), fn, inc, inc_k))
        return tok

    def dma(self, eng, out, in_, wait=(), inc=None, **kw):
        def fn(e):
            return e.dma_start(out=out, in_=in_, **kw)
        return self.do(eng, fn, wait=wait, inc=inc, inc_k=16)

    def realize(self, es):
        for c in self.ctrs:
            c.sem = es.enter_context(self.nc.semaphore(c.name))
        block = es.enter_context(self.nc.Block())
        prog = self

        def emit(eng_name):
            def body(e):
                hw = {}
                for wait, fn, inc, inc_k in prog.steps[eng_name]:
                    merged = {}
                    for (c, v) in wait:
                        if v > merged.get(id(c), (None, 0))[1]:
                            merged[id(c)] = (c, v)
                    for (c, v) in merged.values():
                        if v <= 0 or hw.get(id(c), 0) >= v:
                            continue
                        e.wait_ge(c.sem, v)
                        hw[id(c)] = v
                    ins = fn(e)
                    if inc is not None:
                        if ins is None:
                            ins = e.nop()
                        ins.then_inc(inc.sem, inc_k)

            getattr(block, eng_name)(body)

        for eng_name in ENGS:
            if prog.steps[eng_name]:
                emit(eng_name)
        return block


import contextlib
import numpy as np
import ml_dtypes
import concourse.bass as bass
import concourse.mybir as mybir

F32 = mybir.dt.float32
BF16 = mybir.dt.bfloat16
AF = mybir.ActivationFunctionType
ALU = mybir.AluOpType
AX = mybir.AxisListType

L = 4096
D = 2048
NT = L // 128
KC = D // 128
INV_SQRT_HD = 0.08838834764831845
SEQ_SHIFTS = [0, 1, -1, 3, -3, 9, -9, 26, -26, 78, -78, 232, -232, 689, -689, 2048]
CH_SHIFTS = [1, 2, 4, 8]

ALL_PHASES = ["A0", "A1", "A2", "F0a", "F1a", "F2a", "B0", "B1", "B2", "B3",
              "F0b", "F1b", "F2b"]

SCRATCH = {
    "ln_fm_all": ([8, 128, KC, L], BF16, ["A0"], ["A1"]),
    "ctx_fm":    ([16, 128, L], BF16, ["A1"], ["A2"]),
    "x1":        ([L, D], F32, ["A2"], ["F0a", "F2a"]),
    "x2":        ([L, D], F32, ["F2a"], ["B0", "B3"]),
    "x3":        ([L, D], F32, ["B3"], ["F0b", "F2b"]),
    "ln_fm":     ([128, KC, L], BF16, ["F0a", "B0", "F0b"], ["F1a", "B1", "F1b"]),
    "ln_tm":     ([L, D], BF16, ["F0a", "F0b"], ["F2a", "F2b"]),
    "wfullT_h":  ([12, L], F32, ["F0a", "F0b"], ["F1a", "F1b"]),
    "z_tm":      ([128, NT], F32, ["F0a", "F0b"], ["F2a", "F2b"]),
    "actw_t":    ([NT, 128, 8, 8, 128], BF16, ["F1a", "F1b"], ["F2a", "F2b"]),
    "qmix_tm":   ([L, D], BF16, ["B1"], ["B2"]),
    "kn_tm":     ([L, D], BF16, ["B1"], ["B2"]),
    "v_tm":      ([L, D], BF16, ["B1"], ["B2"]),
    "out4_tm":   ([L, D], BF16, ["B2"], ["B3"]),
}

INPUTS = {
    "x_all":     ([8, L, D], F32),
    "x_own":     ([L, D], F32),
    "wqkvt_sel": ([D, 768], F32),
    "wprojT":    ([D, D], F32),
    "bproj":     ([1, D], F32),
    "clsT":      ([D, 12], F32),
    "corr":      ([1, 12], F32),
    "gateT":     ([8, D, 1024], F32),
    "upT":       ([8, D, 1024], F32),
    "downT":     ([8, 1024, D], F32),
    "wqT":       ([D, D], F32),
    "wkT":       ([D, D], F32),
    "wvT":       ([D, D], F32),
    "woT":       ([D, D], F32),
    "bq":        ([1, D], F32),
    "bk":        ([1, D], F32),
    "bv":        ([1, D], F32),
    "bo":        ([1, D], F32),
    "qnw":       ([1, D], F32),
    "knw":       ([1, D], F32),
    "mix":       ([1, 5], F32),
    "ident_bf":  ([128, 128], BF16),
    "ident_f":   ([128, 128], F32),
    "iota12":    ([128, 12], F32),
    "maskM":     ([128, 2048], BF16),
    "ones1":     ([1, 128], F32),
}

PHASE_INPUTS = {
    "A0": ["x_all", "ident_bf"],
    "A1": ["wqkvt_sel", "ident_bf"],
    "A2": ["wprojT", "bproj", "x_own", "ones1"],
    "F0": ["clsT", "corr", "iota12", "ones1", "ident_bf", "ident_f"],
    "F1": ["gateT", "upT", "ones1"],
    "F2": ["downT"],
    "B0": ["ident_bf"],
    "B1": ["wqT", "wkT", "wvT", "bq", "bk", "bv", "qnw", "knw", "mix", "ones1"],
    "B2": ["maskM", "ident_bf"],
    "B3": ["woT", "bo", "ones1", "ident_bf"],
}


class Bld:
    def __init__(self, nc, phases):
        self.nc = nc
        self.phases = list(phases)
        self.p = Prog(nc)
        self.es = contextlib.ExitStack()
        self.pe = self.p.ctr("pe")
        self.dve = self.p.ctr("dve")
        self.act = self.p.ctr("act")
        self.gp = self.p.ctr("gp")
        self.qs = self.p.ctr("qs")
        self.qc = self.p.ctr("qc")
        self.qv = self.p.ctr("qv")
        self.qg = self.p.ctr("qg")
        self.hbm = {}
        self._nbufctr = 0
        self._nm = 0
        self._ctr_pool = []
        self._phase_ctrs = []
        self.store = self._buf_ctr()
        self.out_toks = []
        self._declare_hbm()
        self.ident = None

    def _buf_ctr(self):
        if self._ctr_pool:
            c = self._ctr_pool.pop()
        else:
            self._nbufctr += 1
            c = self.p.ctr(f"bq{self._nbufctr}")
        self._phase_ctrs.append(c)
        return c

    def _needs_input(self, name):
        for ph in self.phases:
            key = ph[:-1] if (ph not in PHASE_INPUTS and ph[:-1] in PHASE_INPUTS) else ph
            if name in PHASE_INPUTS.get(key, []):
                return True
        return False

    def _declare_hbm(self):
        nc = self.nc
        fused = set(self.phases) == set(ALL_PHASES)
        self.used_inputs = []
        for name, (shape, dt) in INPUTS.items():
            if fused or self._needs_input(name):
                self.hbm[name] = nc.dram_tensor(name, shape, dt, kind="ExternalInput").ap()
                self.used_inputs.append(name)
        self.scratch_kinds = {}
        for name, (shape, dt, prods, cons) in SCRATCH.items():
            produced = any(p in self.phases for p in prods)
            consumed = any(c in self.phases for c in cons)
            if fused:
                kind = "Internal"
            elif produced:
                kind = "ExternalOutput"
            elif consumed:
                kind = "ExternalInput"
            else:
                continue
            self.scratch_kinds[name] = kind
            self.hbm[name] = nc.dram_tensor(name, shape, dt, kind=kind).ap()
        if fused or "F2b" in self.phases:
            self.hbm["x4"] = nc.dram_tensor("x4", [L, D], F32, kind="ExternalOutput").ap()

    def sb(self, es, name, shape, dt=F32):
        self._nm += 1
        t = es.enter_context(self.nc.sbuf_tensor(f"{name}_{self._nm}", shape, dt))
        return t, Buf()

    def ps(self, es, name, shape, dt=F32):
        self._nm += 1
        t = es.enter_context(self.nc.psum_tensor(f"{name}_{self._nm}", shape, dt))
        return t, Buf()

    def barrier_toks(self):
        toks = [c.tok() for c in self.p.ctrs]
        self._ctr_pool.extend(self._phase_ctrs)
        self._phase_ctrs = []
        if self.ident is not None:
            self.ident[1].w = None
            self.ident[1].r = []
            self.ident[1].ctr = None
        self.store = self._buf_ctr()
        return toks

    def dma(self, qname, out, in_, reads=(), writes=(), extra_wait=()):
        if qname == "vector":
            qname = "gpsimd"
        if len(writes) == 1:
            b = writes[0]
        elif len(reads) >= 1:
            b = reads[0]
        else:
            b = None
        if b is not None:
            if b.ctr is None:
                b.ctr = self._buf_ctr()
            ctr = b.ctr
        else:
            ctr = self.store
        return step(self.p, qname, lambda e, out=out, in_=in_: e.dma_start(out=out, in_=in_),
                    reads=reads, writes=writes, inc=ctr, inc_k=16,
                    extra_wait=extra_wait)

    def load_ident(self, es, bar):
        if self.ident is None:
            self._nm += 1
            t = self.es.enter_context(
                self.nc.sbuf_tensor(f"g_ident_{self._nm}", [128, 128], BF16, side="right"))
            b = Buf()
            self.dma("sync", t[:], self.hbm["ident_bf"], writes=[b], extra_wait=bar)
            self.ident = (t, b)
        return self.ident

    def row_tile(self, es, bar, name, src_row_ap, width, out_dt=F32, scale=None, ps=None):
        p = self.p
        srow, srow_b = self.sb(es, name + "_row", [1, width], F32)
        stile, stile_b = self.sb(es, name + "_tile", [128, width], out_dt)
        if ps is None:
            prow, prow_b = self.ps(es, name + "_ps", [128, 512], F32)
        else:
            prow, prow_b = ps
        sones, sones_b = self.sb(es, name + "_ones", [1, 128], F32)
        self.dma("sync", sones[:], self.hbm["ones1"], writes=[sones_b], extra_wait=bar)
        self.dma("sync", srow[:], src_row_ap, writes=[srow_b], extra_wait=bar)
        for j0 in range(0, width, 512):
            w = min(512, width - j0)
            step(p, "tensor",
                 lambda t, j0=j0, w=w: t.matmul(prow[:, 0:w], sones[:], srow[:, j0:j0 + w],
                                                start=True, stop=True),
                 reads=[sones_b, srow_b], writes=[prow_b], inc=self.pe)
            if scale is None:
                step(p, "vector",
                     lambda v, j0=j0, w=w: v.tensor_copy(out=stile[:, j0:j0 + w],
                                                         in_=prow[:, 0:w]),
                     reads=[prow_b], writes=[stile_b], inc=self.dve)
            else:
                step(p, "vector",
                     lambda v, j0=j0, w=w: v.tensor_scalar_mul(out=stile[:, j0:j0 + w],
                                                               in0=prow[:, 0:w], scalar1=scale),
                     reads=[prow_b], writes=[stile_b], inc=self.dve)
        return stile, stile_b

    def emit_ln(self, es, bar, tag, jobs, tile_hook=None):
        p = self.p
        sx, sx_b, sln, sln_b, sfm, sfm_b, ptr, ptr_b = [], [], [], [], [], [], [], []
        for par in range(2):
            t, b = self.sb(es, f"{tag}_x{par}", [128, D], F32); sx.append(t); sx_b.append(b)
            t, b = self.sb(es, f"{tag}_ln{par}", [128, D], BF16); sln.append(t); sln_b.append(b)
            t, b = self.sb(es, f"{tag}_fm{par}", [128, D], BF16); sfm.append(t); sfm_b.append(b)
            t, b = self.ps(es, f"{tag}_ptr{par}", [128, D], BF16); ptr.append(t); ptr_b.append(b)
        sjunk, sjunk_b = self.sb(es, f"{tag}_junk", [128, D], BF16)
        ssum, ssum_b = self.sb(es, f"{tag}_sum", [128, 2], F32)
        sst, sst_b = self.sb(es, f"{tag}_st", [128, 8], F32)
        idt, idt_b = self.load_ident(es, bar)

        for src2d, dst_fm, dst_tm in jobs:
            for T in range(NT):
                par = T % 2
                self.dma("sync", sx[par][:], src2d[T * 128:(T + 1) * 128, :],
                         writes=[sx_b[par]], extra_wait=bar)
                step(p, "vector",
                     lambda v, par=par: v.reduce_sum(out=ssum[:, 0:1], in_=sx[par][:],
                                                     axis=AX.X),
                     reads=[sx_b[par]], writes=[ssum_b], inc=self.dve)
                step(p, "vector",
                     lambda v, par=par: v.scalar_tensor_tensor(
                         out=sjunk[:], in0=sx[par][:], scalar=1.0, in1=sx[par][:],
                         op0=ALU.mult, op1=ALU.mult, accum_out=ssum[:, 1:2]),
                     reads=[sx_b[par]], writes=[sjunk_b, ssum_b], inc=self.dve)
                step(p, "vector",
                     lambda v: v.tensor_scalar_mul(out=sst[:, 0:2], in0=ssum[:, 0:2],
                                                   scalar1=1.0 / D),
                     reads=[ssum_b], writes=[sst_b], inc=self.dve)
                step(p, "vector",
                     lambda v: v.tensor_mul(out=sst[:, 2:3], in0=sst[:, 0:1], in1=sst[:, 0:1]),
                     reads=[sst_b], writes=[sst_b], inc=self.dve)
                step(p, "vector",
                     lambda v: v.tensor_sub(out=sst[:, 3:4], in0=sst[:, 1:2], in1=sst[:, 2:3]),
                     reads=[sst_b], writes=[sst_b], inc=self.dve)
                step(p, "vector",
                     lambda v: v.tensor_scalar_add(out=sst[:, 7:8], in0=sst[:, 3:4],
                                                   scalar1=1e-5),
                     reads=[sst_b], writes=[sst_b], inc=self.dve)
                step(p, "scalar",
                     lambda s: s.activation(out=sst[:, 4:5], in_=sst[:, 7:8], func=AF.Sqrt,
                                            bias=0.0, scale=1.0),
                     reads=[sst_b], writes=[sst_b], inc=self.act)
                step(p, "vector",
                     lambda v: v.reciprocal(out=sst[:, 5:6], in_=sst[:, 4:5]),
                     reads=[sst_b], writes=[sst_b], inc=self.dve)
                step(p, "vector",
                     lambda v: v.scalar_tensor_tensor(out=sst[:, 6:7], in0=sst[:, 0:1],
                                                      scalar=-1.0, in1=sst[:, 5:6],
                                                      op0=ALU.mult, op1=ALU.mult),
                     reads=[sst_b], writes=[sst_b], inc=self.dve)
                step(p, "scalar",
                     lambda s, par=par: s.activation(out=sln[par][:], in_=sx[par][:],
                                                     func=AF.Identity, bias=sst[:, 6:7],
                                                     scale=sst[:, 5:6]),
                     reads=[sx_b[par], sst_b], writes=[sln_b[par]], inc=self.act)
                if dst_tm is not None:
                    self.dma("vector", dst_tm[T * 128:(T + 1) * 128, :], sln[par][:],
                             reads=[sln_b[par]])
                if dst_fm is not None:
                    def tr(t, par=par):
                        ins = None
                        for hc in range(KC):
                            ins = t.transpose(ptr[par][:, hc * 128:(hc + 1) * 128],
                                              sln[par][:, hc * 128:(hc + 1) * 128], idt[:])
                        return ins
                    step(p, "tensor", tr, reads=[sln_b[par], idt_b], writes=[ptr_b[par]],
                         inc=self.pe)
                    step(p, "vector",
                         lambda v, par=par: v.tensor_copy(out=sfm[par][:], in_=ptr[par][:]),
                         reads=[ptr_b[par]], writes=[sfm_b[par]], inc=self.dve)
                    for q in range(4):
                        src3 = sfm[par][:, q * 512:(q + 1) * 512].rearrange(
                            "p (hc t) -> p hc t", t=128)
                        dst3 = dst_fm[:, q * 4:(q + 1) * 4, T * 128:(T + 1) * 128]
                        self.dma("scalar", dst3, src3, reads=[sfm_b[par]])
                    if tile_hook is not None:
                        tile_hook(T, sfm[par], sfm_b[par])

    def emit_A1(self):
        es = contextlib.ExitStack()
        bar = self.barrier_toks()
        p = self.p
        idt, idt_b = self.load_ident(es, bar)

        swst, swst_b = self.sb(es, "a1_wst", [128, 4 * 768], F32)
        swsel, swsel_b = self.sb(es, "a1_wsel", [128, KC * 768], BF16)
        wsrc = self.hbm["wqkvt_sel"].rearrange("(kc p) c -> p kc c", p=128)
        for q4 in range(4):
            self.dma("sync", swst[:].rearrange("p (kc c) -> p kc c", c=768),
                     wsrc[:, q4 * 4:(q4 + 1) * 4, :], writes=[swst_b], extra_wait=bar)
            step(p, "vector",
                 lambda v, q4=q4: v.tensor_copy(out=swsel[:, q4 * 4 * 768:(q4 + 1) * 4 * 768],
                                                in_=swst[:]),
                 reads=[swst_b], writes=[swsel_b], inc=self.dve)

        sln = []
        for par in range(2):
            sln.append(self.sb(es, f"a1_lnr{par}", [128, KC * 512], BF16))
        sqT = [self.sb(es, f"a1_qT{sl}", [128, L], BF16) for sl in range(2)]
        skT = [self.sb(es, f"a1_kT{sl}", [128, L], BF16) for sl in range(2)]
        svtm = [self.sb(es, f"a1_vtm{sl}", [128, L], BF16) for sl in range(2)]
        svfm, svfm_b = self.sb(es, "a1_vfm", [128, L], BF16)
        sp_, sp_b = self.sb(es, "a1_p", [128, 4 * L], BF16)
        spT, spT_b = self.sb(es, "a1_pT", [128, 32 * 512], BF16)
        sls, sls_b = self.sb(es, "a1_ls", [128, 16], F32)
        srec, srec_b = self.sb(es, "a1_rec", [128, 8], F32)
        sctxe, sctxe_b = self.sb(es, "a1_ctxe", [128, 512], BF16)

        pA = self.ps(es, "a1_pA", [128, 512], F32)
        pB = self.ps(es, "a1_pB", [128, 512], F32)
        pS = [self.ps(es, f"a1_pS{i}", [128, 1024], F32) for i in range(2)]
        ppT = [self.ps(es, f"a1_ppT{i}", [128, 1024], BF16) for i in range(2)]

        for m in range(8):
            for grp in range(8):
                par = grp % 2
                lnb, lnb_b = sln[par]
                src = self.hbm["ln_fm_all"][m][:, :, grp * 512:(grp + 1) * 512]
                for h8 in range(2):
                    self.dma("sync",
                             lnb[:, h8 * 8 * 512:(h8 + 1) * 8 * 512].rearrange(
                                 "p (kc t) -> p kc t", t=512),
                             src[:, h8 * 8:(h8 + 1) * 8, :],
                             writes=[lnb_b], extra_wait=bar)
                for sl in range(2):
                    for kind in range(3):
                        pp, pp_b = (pA, pB)[(sl * 3 + kind) % 2]
                        def qkv_mm(t, sl=sl, kind=kind, par=par, pp=pp):
                            lnt, _ = sln[par]
                            ins = None
                            for kc in range(KC):
                                ins = t.matmul(
                                    pp[:],
                                    swsel[:, kc * 768 + sl * 384 + kind * 128:
                                          kc * 768 + sl * 384 + (kind + 1) * 128],
                                    lnt[:, kc * 512:(kc + 1) * 512],
                                    start=(kc == 0), stop=(kc == KC - 1))
                            return ins
                        t_mm = step(p, "tensor", qkv_mm, reads=[swsel_b, lnb_b],
                                    writes=[pp_b], inc=self.pe)
                        dstmap = [sqT[sl], skT[sl], (svfm, svfm_b)]
                        dst, dst_b = dstmap[kind]
                        step(p, "vector",
                             lambda v, dst=dst, grp=grp, pp=pp:
                                 v.tensor_copy(out=dst[:, grp * 512:(grp + 1) * 512],
                                               in_=pp[:]),
                             reads=[pp_b], writes=[dst_b], inc=self.dve)
                    def vtr(t, sl=sl, grp=grp):
                        pt, _ = ppT[grp % 2]
                        ins = None
                        for j in range(4):
                            ins = t.transpose(pt[:, j * 128:(j + 1) * 128],
                                              svfm[:, (grp * 4 + j) * 128:
                                                   (grp * 4 + j + 1) * 128], idt[:])
                        return ins
                    ptb, ptb_b = ppT[grp % 2]
                    step(p, "tensor", vtr, reads=[svfm_b, idt_b], writes=[ptb_b],
                         inc=self.pe)
                    step(p, "vector",
                         lambda v, sl=sl, grp=grp, ptb=ptb:
                             v.tensor_copy(out=svtm[sl][0][:, grp * 512:(grp + 1) * 512],
                                           in_=ptb[:, 0:512]),
                         reads=[ptb_b], writes=[svtm[sl][1]], inc=self.dve)

            for sl in range(2):
                slot = 2 * m + sl
                qTt, qT_b = sqT[sl]
                kTt, kT_b = skT[sl]
                vt, vt_b = svtm[sl]
                for qg in range(8):
                    for ti in range(4):
                        qt0 = qg * 512 + ti * 128
                        for qu in range(4):
                            psq, psq_b = pS[qu % 2]
                            def smm(t, qt0=qt0, qu=qu, psq=psq, qTt=qTt, kTt=kTt):
                                ins = None
                                for h in range(2):
                                    kt0 = qu * 1024 + h * 512
                                    ins = t.matmul(psq[:, h * 512:(h + 1) * 512],
                                                   qTt[:, qt0:qt0 + 128],
                                                   kTt[:, kt0:kt0 + 512],
                                                   start=True, stop=True)
                                return ins
                            step(p, "tensor", smm, reads=[qT_b, kT_b], writes=[psq_b],
                                 inc=self.pe)
                            step(p, "scalar",
                                 lambda s, ti=ti, qu=qu, psq=psq:
                                     s.activation(out=sp_[:, ti * L + qu * 1024:
                                                          ti * L + (qu + 1) * 1024],
                                                  in_=psq[:], func=AF.Exp, bias=0.0,
                                                  scale=INV_SQRT_HD,
                                                  accum_out=sls[:, ti * 4 + qu:
                                                               ti * 4 + qu + 1]),
                                 reads=[psq_b], writes=[sp_b, sls_b], inc=self.act)
                        step(p, "vector",
                             lambda v, ti=ti: v.reduce_sum(out=srec[:, ti:ti + 1],
                                                           in_=sls[:, ti * 4:(ti + 1) * 4],
                                                           axis=AX.X),
                             reads=[sls_b], writes=[srec_b], inc=self.dve)
                        step(p, "vector",
                             lambda v, ti=ti: v.reciprocal(out=srec[:, 4 + ti:5 + ti],
                                                           in_=srec[:, ti:ti + 1]),
                             reads=[srec_b], writes=[srec_b], inc=self.dve)
                        step(p, "vector",
                             lambda v, ti=ti: v.tensor_scalar(
                                 out=sp_[:, ti * L:(ti + 1) * L],
                                 in0=sp_[:, ti * L:(ti + 1) * L],
                                 scalar1=srec[:, 4 + ti:5 + ti], scalar2=None,
                                 op0=ALU.mult),
                             reads=[srec_b], writes=[sp_b], inc=self.dve)
                    for f in range(16):
                        ptb, ptb_b = ppT[f % 2]
                        def ptr_f(t, f=f, ptb=ptb):
                            ins = None
                            for j in range(8):
                                kc = 2 * f + j // 4
                                ti = j % 4
                                ins = t.transpose(ptb[:, j * 128:(j + 1) * 128],
                                                  sp_[:, ti * L + kc * 128:
                                                       ti * L + (kc + 1) * 128], idt[:])
                            return ins
                        step(p, "tensor", ptr_f, reads=[sp_b, idt_b], writes=[ptb_b],
                             inc=self.pe)
                        step(p, "vector",
                             lambda v, f=f, ptb=ptb: v.tensor_copy(
                                 out=spT[:, f * 1024:(f + 1) * 1024], in_=ptb[:]),
                             reads=[ptb_b], writes=[spT_b], inc=self.dve)
                    pc, pc_b = (pA, pB)[qg % 2]
                    def cmm(t, vt=vt, pc=pc):
                        ins = None
                        for kc in range(32):
                            ins = t.matmul(pc[:], vt[:, kc * 128:(kc + 1) * 128],
                                           spT[:, kc * 512:(kc + 1) * 512],
                                           start=(kc == 0), stop=(kc == 31))
                        return ins
                    step(p, "tensor", cmm, reads=[vt_b, spT_b], writes=[pc_b], inc=self.pe)
                    step(p, "vector",
                         lambda v, pc=pc: v.tensor_copy(out=sctxe[:], in_=pc[:]),
                         reads=[pc_b], writes=[sctxe_b], inc=self.dve)
                    self.dma("scalar",
                             self.hbm["ctx_fm"][slot][:, qg * 512:(qg + 1) * 512],
                             sctxe[:], reads=[sctxe_b])
        es.close()

    def emit_A2(self):
        es = contextlib.ExitStack()
        bar = self.barrier_toks()
        p = self.p
        swst, swst_b = self.sb(es, "a2_wst", [128, 4 * D], F32)
        swp, swp_b = self.sb(es, "a2_wp", [128, KC * D], BF16)
        wsrc = self.hbm["wprojT"].rearrange("(hc p) do -> p hc do", p=128)
        for q4 in range(4):
            self.dma("sync", swst[:].rearrange("p (hc do) -> p hc do", do=D),
                     wsrc[:, q4 * 4:(q4 + 1) * 4, :], writes=[swst_b], extra_wait=bar)
            step(p, "vector",
                 lambda v, q4=q4: v.tensor_copy(out=swp[:, q4 * 4 * D:(q4 + 1) * 4 * D],
                                                in_=swst[:]),
                 reads=[swst_b], writes=[swp_b], inc=self.dve)
        sbt, sbt_b = self.row_tile(es, bar, "a2_bp", self.hbm["bproj"], D)
        scg = [self.sb(es, f"a2_cg{i}", [128, 16 * 512], BF16) for i in range(2)]
        sxr = [self.sb(es, f"a2_x{i}", [128, 512], F32) for i in range(2)]
        so1, so1_b = self.sb(es, "a2_o1", [128, 512], F32)
        so2 = [self.sb(es, f"a2_o2{i}", [128, 512], F32) for i in range(2)]
        pO = [self.ps(es, f"a2_pO{i}", [128, 512], F32) for i in range(2)]
        ctx = self.hbm["ctx_fm"]
        for grp in range(8):
            cgt, cg_b = scg[grp % 2]
            self.dma("sync", cgt[:].rearrange("p (s t) -> p s t", t=512),
                     ctx.rearrange("s p t -> p s t")[:, :, grp * 512:(grp + 1) * 512],
                     writes=[cg_b], extra_wait=bar)
            for ti in range(4):
                row0 = grp * 512 + ti * 128
                for dg in range(4):
                    pp, pp_b = pO[dg % 2]
                    xb, xb_b = sxr[dg % 2]
                    ob, ob_b = so2[dg % 2]
                    def omm(t, ti=ti, dg=dg, cgt=cgt, pp=pp):
                        ins = None
                        for hc in range(KC):
                            ins = t.matmul(pp[:],
                                           cgt[:, hc * 512 + ti * 128:hc * 512 + (ti + 1) * 128],
                                           swp[:, hc * D + dg * 512:hc * D + (dg + 1) * 512],
                                           start=(hc == 0), stop=(hc == KC - 1))
                        return ins
                    step(p, "tensor", omm, reads=[cg_b, swp_b], writes=[pp_b], inc=self.pe)
                    self.dma("vector", xb[:],
                             self.hbm["x_own"][row0:row0 + 128, dg * 512:(dg + 1) * 512],
                             writes=[xb_b], extra_wait=bar)
                    step(p, "vector",
                         lambda v, dg=dg, pp=pp: v.scalar_tensor_tensor(
                             out=so1[:], in0=pp[:], scalar=1.0,
                             in1=sbt[:, dg * 512:(dg + 1) * 512],
                             op0=ALU.mult, op1=ALU.add),
                         reads=[pp_b, sbt_b], writes=[so1_b], inc=self.dve)
                    step(p, "vector",
                         lambda v, ob=ob, xb=xb: v.tensor_add(out=ob[:], in0=so1[:], in1=xb[:]),
                         reads=[so1_b, xb_b], writes=[ob_b], inc=self.dve)
                    self.dma("scalar",
                             self.hbm["x1"][row0:row0 + 128, dg * 512:(dg + 1) * 512],
                             ob[:], reads=[ob_b])
        es.close()

    def emit_F0(self, xin, debug=False):
        es = contextlib.ExitStack()
        bar = self.barrier_toks()
        p = self.p
        if debug:
            dbg_lg = self.nc.dram_tensor("dbg_lg", [L, 12], F32, kind="ExternalOutput").ap()
            dbg_pb = self.nc.dram_tensor("dbg_pb", [L, 12], F32, kind="ExternalOutput").ap()
            dbg_bi = self.nc.dram_tensor("dbg_bi", [L, 12], F32, kind="ExternalOutput").ap()
            dbg_m = self.nc.dram_tensor("dbg_m", [L, 24], F32, kind="ExternalOutput").ap()
        sidf, sidf_b = self.sb(es, "f0_idf", [128, 128], F32)
        self.dma("sync", sidf[:], self.hbm["ident_f"], writes=[sidf_b], extra_wait=bar)
        sct, sct_b = self.sb(es, "f0_clsT", [128, KC * 12], BF16)
        scts, scts_b = self.sb(es, "f0_clsTs", [128, KC * 12], F32)
        self.dma("sync", scts[:].rearrange("p (hc e) -> p hc e", e=12),
                 self.hbm["clsT"].rearrange("(hc p) e -> p hc e", p=128),
                 writes=[scts_b], extra_wait=bar)
        step(p, "vector", lambda v: v.tensor_copy(out=sct[:], in_=scts[:]),
             reads=[scts_b], writes=[sct_b], inc=self.dve)
        scorr, scorr_b = self.row_tile(es, bar, "f0_corr", self.hbm["corr"], 12)
        siota, siota_b = self.sb(es, "f0_iota", [128, 12], F32)
        self.dma("sync", siota[:], self.hbm["iota12"], writes=[siota_b], extra_wait=bar)
        slg, slg_b = self.sb(es, "f0_lg", [128, 12], F32)
        srw, srw_b = self.sb(es, "f0_rw", [128, 80], F32)
        swf, swf_b = self.sb(es, "f0_wf", [128, 12], F32)
        swft, swft_b = self.sb(es, "f0_wft", [12, 128], F32)
        szacc, szacc_b = self.sb(es, "f0_zacc", [128, NT], F32)
        pR = self.ps(es, "f0_pR", [128, 12], F32)
        pT12 = self.ps(es, "f0_pT12", [12, 128], F32)

        def router_hook(T, sfmt, sfm_b):
            pRt, pR_b = pR
            def lmm(t, sfmt=sfmt):
                ins = None
                for hc in range(KC):
                    ins = t.matmul(pRt[:], sfmt[:, hc * 128:(hc + 1) * 128],
                                   sct[:, hc * 12:(hc + 1) * 12],
                                   start=(hc == 0), stop=(hc == KC - 1))
                return ins
            step(p, "tensor", lmm, reads=[sfm_b, sct_b], writes=[pR_b], inc=self.pe)
            step(p, "vector", lambda v: v.tensor_copy(out=slg[:], in_=pRt[:]),
                 reads=[pR_b], writes=[slg_b], inc=self.dve)
            step(p, "vector", lambda v: v.reduce_max(out=srw[:, 0:1], in_=slg[:], axis=AX.X),
                 reads=[slg_b], writes=[srw_b], inc=self.dve)
            step(p, "vector", lambda v: v.tensor_scalar_mul(out=srw[:, 1:2], in0=srw[:, 0:1],
                                                            scalar1=-1.0),
                 reads=[srw_b], writes=[srw_b], inc=self.dve)
            step(p, "scalar", lambda s: s.activation(out=srw[:, 16:28], in_=slg[:], func=AF.Exp,
                                                     bias=srw[:, 1:2], scale=1.0,
                                                     accum_out=srw[:, 2:3]),
                 reads=[slg_b, srw_b], writes=[srw_b], inc=self.act)
            step(p, "vector", lambda v: v.reciprocal(out=srw[:, 3:4], in_=srw[:, 2:3]),
                 reads=[srw_b], writes=[srw_b], inc=self.dve)
            step(p, "vector", lambda v: v.tensor_scalar(out=srw[:, 32:44], in0=srw[:, 16:28],
                                                        scalar1=srw[:, 3:4], scalar2=None,
                                                        op0=ALU.mult),
                 reads=[srw_b], writes=[srw_b], inc=self.dve)
            step(p, "vector", lambda v: v.tensor_add(out=srw[:, 44:56], in0=srw[:, 32:44],
                                                     in1=scorr[:, 0:12]),
                 reads=[srw_b, scorr_b], writes=[srw_b], inc=self.dve)
            if debug:
                self.dma("scalar", dbg_pb[T * 128:(T + 1) * 128, :], srw[:, 32:44],
                         reads=[srw_b])
                self.dma("scalar", dbg_bi[T * 128:(T + 1) * 128, :], srw[:, 44:56],
                         reads=[srw_b])
            step(p, "vector", lambda v: v.reduce_max(out=srw[:, 4:5], in_=srw[:, 44:56],
                                                     axis=AX.X),
                 reads=[srw_b], writes=[srw_b], inc=self.dve)
            step(p, "vector", lambda v: v.tensor_scalar(out=srw[:, 16:28], in0=srw[:, 44:56],
                                                        scalar1=srw[:, 4:5], scalar2=None,
                                                        op0=ALU.is_ge),
                 reads=[srw_b], writes=[srw_b], inc=self.dve)
            step(p, "vector", lambda v: v.scalar_tensor_tensor(out=srw[:, 16:28],
                                                               in0=srw[:, 16:28], scalar=-1e4,
                                                               in1=siota[:], op0=ALU.mult,
                                                               op1=ALU.add),
                 reads=[srw_b, siota_b], writes=[srw_b], inc=self.dve)
            step(p, "vector", lambda v: v.tensor_scalar_add(out=srw[:, 16:28],
                                                            in0=srw[:, 16:28], scalar1=1e4),
                 reads=[srw_b], writes=[srw_b], inc=self.dve)
            step(p, "vector", lambda v: v.tensor_reduce(out=srw[:, 5:6], in_=srw[:, 16:28],
                                                        axis=AX.X, op=ALU.min),
                 reads=[srw_b], writes=[srw_b], inc=self.dve)
            step(p, "vector", lambda v: v.tensor_scalar(out=srw[:, 16:28], in0=siota[:],
                                                        scalar1=srw[:, 5:6], scalar2=None,
                                                        op0=ALU.is_equal),
                 reads=[srw_b, siota_b], writes=[srw_b], inc=self.dve)
            step(p, "vector", lambda v: v.scalar_tensor_tensor(out=srw[:, 56:68],
                                                               in0=srw[:, 16:28], scalar=-1e30,
                                                               in1=srw[:, 44:56], op0=ALU.mult,
                                                               op1=ALU.add),
                 reads=[srw_b], writes=[srw_b], inc=self.dve)
            step(p, "vector", lambda v: v.tensor_copy(out=srw[:, 68:80], in_=srw[:, 16:28]),
                 reads=[srw_b], writes=[srw_b], inc=self.dve)
            step(p, "vector", lambda v: v.reduce_max(out=srw[:, 6:7], in_=srw[:, 56:68],
                                                     axis=AX.X),
                 reads=[srw_b], writes=[srw_b], inc=self.dve)
            step(p, "vector", lambda v: v.tensor_scalar(out=srw[:, 16:28], in0=srw[:, 56:68],
                                                        scalar1=srw[:, 6:7], scalar2=None,
                                                        op0=ALU.is_ge),
                 reads=[srw_b], writes=[srw_b], inc=self.dve)
            step(p, "vector", lambda v: v.scalar_tensor_tensor(out=srw[:, 16:28],
                                                               in0=srw[:, 16:28], scalar=-1e4,
                                                               in1=siota[:], op0=ALU.mult,
                                                               op1=ALU.add),
                 reads=[srw_b, siota_b], writes=[srw_b], inc=self.dve)
            step(p, "vector", lambda v: v.tensor_scalar_add(out=srw[:, 16:28],
                                                            in0=srw[:, 16:28], scalar1=1e4),
                 reads=[srw_b], writes=[srw_b], inc=self.dve)
            step(p, "vector", lambda v: v.tensor_reduce(out=srw[:, 7:8], in_=srw[:, 16:28],
                                                        axis=AX.X, op=ALU.min),
                 reads=[srw_b], writes=[srw_b], inc=self.dve)
            step(p, "vector", lambda v: v.tensor_scalar(out=srw[:, 16:28], in0=siota[:],
                                                        scalar1=srw[:, 7:8], scalar2=None,
                                                        op0=ALU.is_equal),
                 reads=[srw_b, siota_b], writes=[srw_b], inc=self.dve)
            step(p, "vector", lambda v: v.tensor_add(out=srw[:, 16:28], in0=srw[:, 16:28],
                                                     in1=srw[:, 68:80]),
                 reads=[srw_b], writes=[srw_b], inc=self.dve)
            if debug:
                self.dma("scalar", dbg_lg[T * 128:(T + 1) * 128, :], slg[:], reads=[slg_b])
                self.dma("scalar", dbg_m[T * 128:(T + 1) * 128, 0:12], srw[:, 16:28],
                         reads=[srw_b])
                self.dma("scalar", dbg_m[T * 128:(T + 1) * 128, 12:24], srw[:, 68:80],
                         reads=[srw_b])
            step(p, "vector", lambda v: v.scalar_tensor_tensor(out=swf[:], in0=srw[:, 32:44],
                                                               scalar=1.5, in1=srw[:, 16:28],
                                                               op0=ALU.mult, op1=ALU.mult),
                 reads=[srw_b], writes=[swf_b], inc=self.dve)
            step(p, "vector", lambda v: v.tensor_reduce(out=srw[:, 8:9], in_=swf[:, 8:12],
                                                        axis=AX.X, op=ALU.add),
                 reads=[swf_b], writes=[srw_b], inc=self.dve)
            step(p, "vector", lambda v, T=T: v.tensor_copy(out=szacc[:, T:T + 1],
                                                            in_=srw[:, 8:9]),
                 reads=[srw_b], writes=[szacc_b], inc=self.dve)
            pTt, pT_b = pT12
            step(p, "tensor", lambda t: t.transpose(pTt[:], swf[:], sidf[:]),
                 reads=[swf_b, sidf_b], writes=[pT_b], inc=self.pe)
            step(p, "vector", lambda v: v.tensor_copy(out=swft[:], in_=pTt[:]),
                 reads=[pT_b], writes=[swft_b], inc=self.dve)
            self.dma("vector", self.hbm["wfullT_h"][:, T * 128:(T + 1) * 128], swft[:],
                     reads=[swft_b])

        jobs = [(self.hbm[xin], self.hbm["ln_fm"], self.hbm["ln_tm"])]
        self.emit_ln(es, bar, "f0", jobs, tile_hook=router_hook)
        self.dma("scalar", self.hbm["z_tm"], szacc[:], reads=[szacc_b])
        es.close()

    def emit_F1(self):
        es = contextlib.ExitStack()
        bar = self.barrier_toks()
        p = self.p
        sones, sones_b = self.sb(es, "f1_ones", [1, 128], F32)
        self.dma("sync", sones[:], self.hbm["ones1"], writes=[sones_b], extra_wait=bar)
        swrow = [self.sb(es, "f1_wrow0", [1, L], F32)] * 2
        swtile = [self.sb(es, "f1_wtile0", [128, L], BF16)] * 2
        pW = self.ps(es, "f1_pW", [128, 512], F32)
        sgT = [self.sb(es, f"f1_gT{i}", [128, KC * 1024], BF16) for i in range(2)]
        suT = [self.sb(es, f"f1_uT{i}", [128, KC * 1024], BF16) for i in range(2)]
        swst = [self.sb(es, "f1_wst0", [128, 2 * 1024], F32)] * 2
        sln = [self.sb(es, f"f1_lnr{i}", [128, KC * 512], BF16) for i in range(2)]
        ssg, ssg_b = self.sb(es, "f1_sg", [128, 512], BF16)
        st1, st1_b = self.sb(es, "f1_t1", [128, 512], BF16)
        saw, saw_b = self.sb(es, "f1_aw", [128, 8 * 512], BF16)
        pG = [self.ps(es, f"f1_pG{i}", [128, 512], F32) for i in range(2)]
        pU = [self.ps(es, f"f1_pU{i}", [128, 512], F32) for i in range(2)]

        def build_wtile(e):
            pWt, pW_b = pW
            srow, srow_b = swrow[0]
            stl, stl_b = swtile[0]
            self.dma("scalar", srow[:], self.hbm["wfullT_h"][e:e + 1, :],
                     writes=[srow_b], extra_wait=bar)
            for j in range(8):
                step(p, "tensor",
                     lambda t, j=j, srow=srow, pWt=pWt: t.matmul(
                         pWt[:], sones[:], srow[:, j * 512:(j + 1) * 512],
                         start=True, stop=True),
                     reads=[sones_b, srow_b], writes=[pW_b], inc=self.pe)
                step(p, "vector",
                     lambda v, j=j, stl=stl, pWt=pWt: v.tensor_copy(
                         out=stl[:, j * 512:(j + 1) * 512], in_=pWt[:]),
                     reads=[pW_b], writes=[stl_b], inc=self.dve)

        def load_weights(e):
            par = e % 2
            stg, stg_b = swst[0]
            for w_hbm, (dst, dst_b) in ((self.hbm["gateT"], sgT[par]),
                                        (self.hbm["upT"], suT[par])):
                wsrc = w_hbm[e].rearrange("(kc p) i -> p kc i", p=128)
                for h8 in range(8):
                    self.dma("scalar", stg[:].rearrange("p (kc i) -> p kc i", i=1024),
                             wsrc[:, h8 * 2:(h8 + 1) * 2, :], writes=[stg_b],
                             extra_wait=bar)
                    step(p, "vector",
                         lambda v, h8=h8, dst=dst, stg=stg: v.tensor_copy(
                             out=dst[:, h8 * 2 * 1024:(h8 + 1) * 2 * 1024], in_=stg[:]),
                         reads=[stg_b], writes=[dst_b], inc=self.dve)

        load_weights(0)
        for e in range(8):
            build_wtile(e)
            if e + 1 < 8:
                load_weights(e + 1)
            par = e % 2
            sgTt, sgT_b = sgT[par]
            suTt, suT_b = suT[par]
            stl, stl_b = swtile[0]
            for grp in range(8):
                lnb, lnb_b = sln[grp % 2]
                src = self.hbm["ln_fm"][:, :, grp * 512:(grp + 1) * 512]
                for h8 in range(2):
                    self.dma("sync",
                             lnb[:, h8 * 8 * 512:(h8 + 1) * 8 * 512].rearrange(
                                 "p (kc t) -> p kc t", t=512),
                             src[:, h8 * 8:(h8 + 1) * 8, :], writes=[lnb_b],
                             extra_wait=bar)
                for ic in range(8):
                    pGt, pG_b = pG[ic % 2]
                    pUt, pU_b = pU[ic % 2]
                    def gmm(t, ic=ic, lnb=lnb, wt=sgTt, pp=pGt):
                        ins = None
                        for kc in range(KC):
                            ins = t.matmul(pp[:],
                                           wt[:, kc * 1024 + ic * 128:kc * 1024 + (ic + 1) * 128],
                                           lnb[:, kc * 512:(kc + 1) * 512],
                                           start=(kc == 0), stop=(kc == KC - 1))
                        return ins
                    step(p, "tensor", gmm, reads=[sgT_b, lnb_b], writes=[pG_b], inc=self.pe)
                    def umm(t, ic=ic, lnb=lnb, wt=suTt, pp=pUt):
                        ins = None
                        for kc in range(KC):
                            ins = t.matmul(pp[:],
                                           wt[:, kc * 1024 + ic * 128:kc * 1024 + (ic + 1) * 128],
                                           lnb[:, kc * 512:(kc + 1) * 512],
                                           start=(kc == 0), stop=(kc == KC - 1))
                        return ins
                    step(p, "tensor", umm, reads=[suT_b, lnb_b], writes=[pU_b], inc=self.pe)
                    step(p, "scalar",
                         lambda s, pGt=pGt: s.activation(out=ssg[:], in_=pGt[:], func=AF.Silu,
                                                         bias=0.0, scale=1.0),
                         reads=[pG_b], writes=[ssg_b], inc=self.act)
                    step(p, "vector",
                         lambda v, pUt=pUt: v.tensor_tensor(out=st1[:], in0=pUt[:], in1=ssg[:],
                                                            op=ALU.mult),
                         reads=[pU_b, ssg_b], writes=[st1_b], inc=self.dve)
                    step(p, "gpsimd",
                         lambda g, ic=ic, grp=grp, stl=stl: g.tensor_tensor(
                             out=saw[:, ic * 512:(ic + 1) * 512], in0=st1[:],
                             in1=stl[:, grp * 512:(grp + 1) * 512], op=ALU.mult),
                         reads=[st1_b, stl_b], writes=[saw_b], inc=self.gp)
                for tsub in range(4):
                    tile = grp * 4 + tsub
                    src3 = saw[:].rearrange("p (ic ts t) -> p ic ts t", ts=4, t=128)[
                        :, :, tsub, :]
                    dst3 = self.hbm["actw_t"][tile, :, e, :, :]
                    self.dma("gpsimd", dst3, src3, reads=[saw_b])
        es.close()

    def emit_F2(self, xin, xout):
        es = contextlib.ExitStack()
        bar = self.barrier_toks()
        p = self.p
        sdT = [self.sb(es, f"f2_dT{i}", [128, 64 * 512], BF16) for i in range(2)]
        sdst = [self.sb(es, f"f2_dst{i}", [128, 4 * 512], F32) for i in range(2)]
        szall, szall_b = self.sb(es, "f2_z", [128, NT], F32)
        self.dma("sync", szall[:], self.hbm["z_tm"], writes=[szall_b], extra_wait=bar)
        sat = [self.sb(es, f"f2_at{i}", [128, 64 * 128], BF16) for i in range(2)]
        slnr = [self.sb(es, f"f2_ln{i}", [128, 512], BF16) for i in range(2)]
        sxr = [self.sb(es, f"f2_x{i}", [128, 512], F32) for i in range(2)]
        so1, so1_b = self.sb(es, "f2_o1", [128, 512], F32)
        so2 = [self.sb(es, f"f2_o2{i}", [128, 512], F32) for i in range(2)]
        pD = [self.ps(es, f"f2_pD{i}", [128, 512], F32) for i in range(2)]

        def load_dg(dg):
            par = dg % 2
            dstt, dst_b = sdT[par]
            stg, stg_b = sdst[par]
            for e in range(8):
                for h2 in range(2):
                    dsrc = self.hbm["downT"][e].rearrange("(ic p) do -> p ic do", p=128)[
                        :, h2 * 4:(h2 + 1) * 4, dg * 512:(dg + 1) * 512]
                    self.dma("scalar", stg[:].rearrange("p (ic do) -> p ic do", do=512),
                             dsrc, writes=[stg_b], extra_wait=bar)
                    step(p, "vector",
                         lambda v, e=e, h2=h2, dstt=dstt, stg=stg: v.tensor_copy(
                             out=dstt[:, e * 4096 + h2 * 2048:e * 4096 + (h2 + 1) * 2048],
                             in_=stg[:]),
                         reads=[stg_b], writes=[dst_b], inc=self.dve)

        load_dg(0)
        for dg in range(4):
            if dg + 1 < 4:
                load_dg(dg + 1)
            sdTt, sdT_b = sdT[dg % 2]
            for tile in range(NT):
                att, at_b = sat[tile % 2]
                self.dma("sync", att[:].rearrange("p (ec t) -> p ec t", t=128),
                         self.hbm["actw_t"][tile].rearrange("p e ic t -> p (e ic) t"),
                         writes=[at_b], extra_wait=bar)
                pp, pp_b = pD[tile % 2]
                def dmm(t, att=att, pp=pp, sdTt=sdTt):
                    ins = None
                    for ec in range(64):
                        ins = t.matmul(pp[:], att[:, ec * 128:(ec + 1) * 128],
                                       sdTt[:, ec * 512:(ec + 1) * 512],
                                       start=(ec == 0), stop=(ec == 63))
                    return ins
                step(p, "tensor", dmm, reads=[at_b, sdT_b], writes=[pp_b], inc=self.pe)
                lnb, lnb_b = slnr[tile % 2]
                xb, xb_b = sxr[tile % 2]
                ob, ob_b = so2[tile % 2]
                self.dma("vector", lnb[:],
                         self.hbm["ln_tm"][tile * 128:(tile + 1) * 128,
                                           dg * 512:(dg + 1) * 512],
                         writes=[lnb_b], extra_wait=bar)
                self.dma("vector", xb[:],
                         self.hbm[xin][tile * 128:(tile + 1) * 128, dg * 512:(dg + 1) * 512],
                         writes=[xb_b], extra_wait=bar)
                step(p, "vector",
                     lambda v, tile=tile, lnb=lnb, pp=pp: v.scalar_tensor_tensor(
                         out=so1[:], in0=lnb[:], scalar=szall[:, tile:tile + 1],
                         in1=pp[:], op0=ALU.mult, op1=ALU.add),
                     reads=[lnb_b, pp_b, szall_b], writes=[so1_b], inc=self.dve)
                step(p, "vector",
                     lambda v, ob=ob, xb=xb: v.tensor_add(out=ob[:], in0=so1[:], in1=xb[:]),
                     reads=[so1_b, xb_b], writes=[ob_b], inc=self.dve)
                self.dma("scalar",
                         self.hbm[xout][tile * 128:(tile + 1) * 128, dg * 512:(dg + 1) * 512],
                         ob[:], reads=[ob_b])
        es.close()

    def emit_A0(self):
        es = contextlib.ExitStack()
        bar = self.barrier_toks()
        jobs = [(self.hbm["x_all"][b], self.hbm["ln_fm_all"][b], None) for b in range(8)]
        self.emit_ln(es, bar, "a0", jobs)
        es.close()

    def emit_B0(self):
        es = contextlib.ExitStack()
        bar = self.barrier_toks()
        self.emit_ln(es, bar, "b0", [(self.hbm["x2"], self.hbm["ln_fm"], None)])
        es.close()

    def emit_B1(self):
        es = contextlib.ExitStack()
        bar = self.barrier_toks()
        p = self.p
        pQ = [self.ps(es, f"b1_pQ{i}", [128, D], F32) for i in range(2)]
        pRT = (pQ[0][0][:, 0:512], pQ[0][1])
        sbqt, sbq_b = self.row_tile(es, bar, "b1_bq", self.hbm["bq"], D, out_dt=BF16, ps=pRT)
        sbkt, sbk_b = self.row_tile(es, bar, "b1_bk", self.hbm["bk"], D, out_dt=BF16, ps=pRT)
        sbvt, sbv_b = self.row_tile(es, bar, "b1_bv", self.hbm["bv"], D, out_dt=BF16, ps=pRT)
        sqnw, sqnw_b = self.row_tile(es, bar, "b1_qnw", self.hbm["qnw"], D, out_dt=BF16, ps=pRT)
        sknw, sknw_b = self.row_tile(es, bar, "b1_knw", self.hbm["knw"], D, out_dt=BF16, ps=pRT)
        smix, smix_b = self.row_tile(es, bar, "b1_mix", self.hbm["mix"], 5, ps=pRT)

        swst, swst_b = self.sb(es, "b1_wst", [128, 2 * D], F32)
        swT, swT_b = self.sb(es, "b1_wT", [128, KC * D], BF16)
        sln = [self.sb(es, f"b1_lnr{i}", [128, KC * 512], BF16) for i in range(2)]
        sqf, sqf_b = self.sb(es, "b1_qf", [128, D], F32)
        sqn, sqn_b = self.sb(es, "b1_qn", [128, D], BF16)
        sqm, sqm_b = self.sb(es, "b1_qm", [128, D], BF16)
        sjunk, sjunk_b = self.sb(es, "b1_junk", [128, D], BF16)
        sst, sst_b = self.sb(es, "b1_st", [128, 4], F32)

        kinds = [("q", self.hbm["wqT"], sbqt, sbq_b, self.hbm["qmix_tm"]),
                 ("k", self.hbm["wkT"], sbkt, sbk_b, self.hbm["kn_tm"]),
                 ("v", self.hbm["wvT"], sbvt, sbv_b, self.hbm["v_tm"])]
        for kind, w_hbm, btile, btile_b, dst_hbm in kinds:
            wsrc = w_hbm.rearrange("(kc p) do -> p kc do", p=128)
            for q8 in range(8):
                self.dma("sync", swst[:].rearrange("p (kc do) -> p kc do", do=D),
                         wsrc[:, q8 * 2:(q8 + 1) * 2, :], writes=[swst_b], extra_wait=bar)
                step(p, "vector",
                     lambda v, q8=q8: v.tensor_copy(out=swT[:, q8 * 2 * D:(q8 + 1) * 2 * D],
                                                    in_=swst[:]),
                     reads=[swst_b], writes=[swT_b], inc=self.dve)
            for grp in range(8):
                par = grp % 2
                lnb, lnb_b = sln[par]
                src = self.hbm["ln_fm"][:, :, grp * 512:(grp + 1) * 512]
                for h8 in range(2):
                    self.dma("sync",
                             lnb[:, h8 * 8 * 512:(h8 + 1) * 8 * 512].rearrange(
                                 "p (kc t) -> p kc t", t=512),
                             src[:, h8 * 8:(h8 + 1) * 8, :], writes=[lnb_b],
                             extra_wait=bar)
                for ti in range(4):
                    tile = grp * 4 + ti
                    pQt, pQ_b = pQ[ti % 2]
                    def pmm(t, ti=ti, lnb=lnb, pQt=pQt):
                        ins = None
                        for dg in range(4):
                            for kc in range(KC):
                                ins = t.matmul(
                                    pQt[:, dg * 512:(dg + 1) * 512],
                                    lnb[:, kc * 512 + ti * 128:kc * 512 + (ti + 1) * 128],
                                    swT[:, kc * D + dg * 512:kc * D + (dg + 1) * 512],
                                    start=(kc == 0), stop=(kc == KC - 1))
                        return ins
                    step(p, "tensor", pmm, reads=[lnb_b, swT_b], writes=[pQ_b], inc=self.pe)
                    if kind == "v":
                        step(p, "vector",
                             lambda v, pQt=pQt, btile=btile: v.scalar_tensor_tensor(
                                 out=sqn[:], in0=pQt[:], scalar=1.0, in1=btile[:],
                                 op0=ALU.mult, op1=ALU.add),
                             reads=[pQ_b, btile_b], writes=[sqn_b], inc=self.dve)
                        self.dma("scalar", dst_hbm[tile * 128:(tile + 1) * 128, :], sqn[:],
                                 reads=[sqn_b])
                        continue
                    step(p, "vector",
                         lambda v, pQt=pQt, btile=btile: v.tensor_add(out=sqf[:], in0=pQt[:],
                                                                      in1=btile[:]),
                         reads=[pQ_b, btile_b], writes=[sqf_b], inc=self.dve)
                    step(p, "vector",
                         lambda v: v.scalar_tensor_tensor(out=sjunk[:], in0=sqf[:], scalar=1.0,
                                                          in1=sqf[:], op0=ALU.mult,
                                                          op1=ALU.mult,
                                                          accum_out=sst[:, 0:1]),
                         reads=[sqf_b], writes=[sjunk_b, sst_b], inc=self.dve)
                    step(p, "vector",
                         lambda v: v.tensor_scalar(out=sst[:, 1:2], in0=sst[:, 0:1],
                                                   scalar1=1.0 / D, scalar2=1e-6,
                                                   op0=ALU.mult, op1=ALU.add),
                         reads=[sst_b], writes=[sst_b], inc=self.dve)
                    step(p, "scalar",
                         lambda s: s.activation(out=sst[:, 2:3], in_=sst[:, 1:2],
                                                func=AF.Sqrt, bias=0.0, scale=1.0),
                         reads=[sst_b], writes=[sst_b], inc=self.act)
                    step(p, "vector",
                         lambda v: v.reciprocal(out=sst[:, 3:4], in_=sst[:, 2:3]),
                         reads=[sst_b], writes=[sst_b], inc=self.dve)
                    gmt, gmt_b = (sqnw, sqnw_b) if kind == "q" else (sknw, sknw_b)
                    step(p, "vector",
                         lambda v, gmt=gmt: v.scalar_tensor_tensor(
                             out=sqn[:], in0=sqf[:], scalar=sst[:, 3:4], in1=gmt[:],
                             op0=ALU.mult, op1=ALU.mult),
                         reads=[sqf_b, sst_b, gmt_b], writes=[sqn_b], inc=self.dve)
                    if kind == "k":
                        self.dma("scalar", dst_hbm[tile * 128:(tile + 1) * 128, :], sqn[:],
                                 reads=[sqn_b])
                        continue
                    step(p, "vector",
                         lambda v: v.tensor_scalar(out=sqm[:], in0=sqn[:],
                                                   scalar1=smix[:, 0:1], scalar2=None,
                                                   op0=ALU.mult),
                         reads=[sqn_b, smix_b], writes=[sqm_b], inc=self.dve)
                    qn3 = sqn[:].rearrange("p (h d) -> p h d", d=128)
                    qm3 = sqm[:].rearrange("p (h d) -> p h d", d=128)
                    for n, c in enumerate(CH_SHIFTS):
                        step(p, "vector",
                             lambda v, n=n, c=c: v.scalar_tensor_tensor(
                                 out=qm3[:, :, c:128], in0=qn3[:, :, 0:128 - c],
                                 scalar=smix[:, n + 1:n + 2], in1=qm3[:, :, c:128],
                                 op0=ALU.mult, op1=ALU.add),
                             reads=[sqn_b, smix_b, sqm_b], writes=[sqm_b], inc=self.dve)
                        step(p, "vector",
                             lambda v, n=n, c=c: v.scalar_tensor_tensor(
                                 out=qm3[:, :, 0:c], in0=qn3[:, :, 128 - c:128],
                                 scalar=smix[:, n + 1:n + 2], in1=qm3[:, :, 0:c],
                                 op0=ALU.mult, op1=ALU.add),
                             reads=[sqn_b, smix_b, sqm_b], writes=[sqm_b], inc=self.dve)
                    self.dma("scalar", dst_hbm[tile * 128:(tile + 1) * 128, :], sqm[:],
                             reads=[sqm_b])
        es.close()

    def emit_B2(self):
        es = contextlib.ExitStack()
        bar = self.barrier_toks()
        p = self.p
        idt, idt_b = self.load_ident(es, bar)
        smask, smask_b = self.sb(es, "b2_mask", [128, 2048], BF16)
        self.dma("sync", smask[:], self.hbm["maskM"], writes=[smask_b], extra_wait=bar)
        EXT = 6848
        skext, skext_b = self.sb(es, "b2_kext", [128, 4 * EXT], BF16)
        sqfm, sqfm_b = self.sb(es, "b2_qfm", [128, 4 * L], BF16)
        skt, skt_b = self.sb(es, "b2_kt", [128, 512], BF16)
        sqt, sqt_b = self.sb(es, "b2_qt", [128, 512], BF16)
        svsh = [self.sb(es, f"b2_vsh{i}", [128, 16 * 512], BF16) for i in range(2)]
        stexp, stexp_b = self.sb(es, "b2_texp", [128, 2048], BF16)
        sme, sme_b = self.sb(es, "b2_me", [128, 2048], BF16)
        sden, sden_b = self.sb(es, "b2_den", [128, 4], F32)
        srec, srec_b = self.sb(es, "b2_rec", [128, 4], F32)
        so4, so4_b = self.sb(es, "b2_o4", [128, 512], BF16)
        pST = [self.ps(es, f"b2_pST{i}", [128, 1024], F32) for i in range(2)]
        pO4 = [self.ps(es, f"b2_pO4{i}", [128, 512], F32) for i in range(2)]
        ppTb = self.ps(es, "b2_ppTb", [128, 512], BF16)

        for hg in range(4):
            for srcname, dstt, dst_b, base, width in (
                    ("kn_tm", skext, skext_b, 2048, EXT),
                    ("qmix_tm", sqfm, sqfm_b, 0, L)):
                stg, stg_b = (skt, skt_b) if srcname == "kn_tm" else (sqt, sqt_b)
                for T in range(NT):
                    self.dma("sync", stg[:],
                             self.hbm[srcname][T * 128:(T + 1) * 128,
                                               hg * 512:(hg + 1) * 512],
                             writes=[stg_b], extra_wait=bar)
                    pt, pt_b = ppTb
                    def ktr(t, stg=stg, pt=pt):
                        ins = None
                        for h in range(4):
                            ins = t.transpose(pt[:, h * 128:(h + 1) * 128],
                                              stg[:, h * 128:(h + 1) * 128], idt[:])
                        return ins
                    step(p, "tensor", ktr, reads=[stg_b, idt_b], writes=[pt_b], inc=self.pe)
                    dview = dstt[:].rearrange("p (h c) -> p h c", c=width)[
                        :, :, base + T * 128:base + (T + 1) * 128]
                    step(p, "vector",
                         lambda v, dview=dview, pt=pt: v.tensor_copy(
                             out=dview, in_=pt[:].rearrange("p (h c) -> p h c", c=128)),
                         reads=[pt_b], writes=[dst_b], inc=self.dve)
            kv = skext[:].rearrange("p (h c) -> p h c", c=EXT)
            for h in range(4):
                self.dma("sync", kv[:, h, 0:2048], kv[:, h, 4096:6144],
                         reads=[skext_b], writes=[skext_b], extra_wait=bar)
                self.dma("sync", kv[:, h, 6144:6848], kv[:, h, 2048:2752],
                         reads=[skext_b], writes=[skext_b], extra_wait=bar)

            for T in range(NT):
                vb, vb_b = svsh[T % 2]
                l0 = T * 128
                for i, sft in enumerate(SEQ_SHIFTS):
                    r0 = (l0 - sft) % L
                    n1 = min(128, L - r0)
                    self.dma("sync", vb[:n1, i * 512:(i + 1) * 512],
                             self.hbm["v_tm"][r0:r0 + n1, hg * 512:(hg + 1) * 512],
                             writes=[vb_b], extra_wait=bar)
                    if n1 < 128:
                        self.dma("sync", vb[n1:128, i * 512:(i + 1) * 512],
                                 self.hbm["v_tm"][0:128 - n1, hg * 512:(hg + 1) * 512],
                                 writes=[vb_b], extra_wait=bar)
                po, po_b = pO4[T % 2]
                for h in range(4):
                    for hf in range(2):
                        ps_, ps_b = pST[hf]
                        def stmm(t, h=h, hf=hf, l0=l0, ps_=ps_):
                            ins = None
                            for j in range(8):
                                sft = SEQ_SHIFTS[hf * 8 + j]
                                col = h * EXT + 2048 + l0 - sft
                                ins = t.matmul(ps_[:, j * 128:(j + 1) * 128],
                                               skext[:, col:col + 128],
                                               sqfm[:, h * L + l0:h * L + l0 + 128],
                                               start=True, stop=True)
                            return ins
                        step(p, "tensor", stmm, reads=[skext_b, sqfm_b], writes=[ps_b],
                             inc=self.pe)
                        step(p, "scalar",
                             lambda s, hf=hf, ps_=ps_: s.activation(
                                 out=stexp[:, hf * 1024:(hf + 1) * 1024], in_=ps_[:],
                                 func=AF.Exp, bias=0.0, scale=INV_SQRT_HD),
                             reads=[ps_b], writes=[stexp_b], inc=self.act)
                    for hf in range(2):
                        step(p, "vector",
                             lambda v, hf=hf: v.scalar_tensor_tensor(
                                 out=sme[:, hf * 1024:(hf + 1) * 1024],
                                 in0=stexp[:, hf * 1024:(hf + 1) * 1024], scalar=1.0,
                                 in1=smask[:, hf * 1024:(hf + 1) * 1024],
                                 op0=ALU.mult, op1=ALU.mult,
                                 accum_out=sden[:, hf:hf + 1]),
                             reads=[stexp_b, smask_b], writes=[sme_b, sden_b],
                             inc=self.dve)
                    step(p, "vector",
                         lambda v: v.tensor_add(out=sden[:, 2:3], in0=sden[:, 0:1],
                                                in1=sden[:, 1:2]),
                         reads=[sden_b], writes=[sden_b], inc=self.dve)
                    step(p, "vector",
                         lambda v, h=h: v.reciprocal(out=srec[:, h:h + 1],
                                                     in_=sden[:, 2:3]),
                         reads=[sden_b], writes=[srec_b], inc=self.dve)
                    def vmm(t, h=h, po=po, vb=vb):
                        ins = None
                        for i in range(16):
                            ins = t.matmul(po[:, h * 128:(h + 1) * 128],
                                           sme[:, i * 128:(i + 1) * 128],
                                           vb[:, i * 512 + h * 128:i * 512 + (h + 1) * 128],
                                           start=(i == 0), stop=(i == 15))
                        return ins
                    step(p, "tensor", vmm, reads=[sme_b, vb_b], writes=[po_b], inc=self.pe)
                for h in range(4):
                    step(p, "vector",
                         lambda v, h=h, po=po: v.tensor_scalar(
                             out=so4[:, h * 128:(h + 1) * 128],
                             in0=po[:, h * 128:(h + 1) * 128],
                             scalar1=srec[:, h:h + 1], scalar2=None, op0=ALU.mult),
                         reads=[po_b, srec_b], writes=[so4_b], inc=self.dve)
                self.dma("scalar",
                         self.hbm["out4_tm"][l0:l0 + 128, hg * 512:(hg + 1) * 512],
                         so4[:], reads=[so4_b])
        es.close()

    def emit_B3(self):
        es = contextlib.ExitStack()
        bar = self.barrier_toks()
        p = self.p
        idt, idt_b = self.load_ident(es, bar)
        pRT = self.ps(es, "b3_pRT", [128, 512], F32)
        sbot, sbo_b = self.row_tile(es, bar, "b3_bo", self.hbm["bo"], D, ps=pRT)
        swst, swst_b = self.sb(es, "b3_wst", [128, 4 * D], F32)
        swo, swo_b = self.sb(es, "b3_wo", [128, KC * D], BF16)
        wsrc = self.hbm["woT"].rearrange("(hc p) do -> p hc do", p=128)
        for q4 in range(4):
            self.dma("sync", swst[:].rearrange("p (hc do) -> p hc do", do=D),
                     wsrc[:, q4 * 4:(q4 + 1) * 4, :], writes=[swst_b], extra_wait=bar)
            step(p, "vector",
                 lambda v, q4=q4: v.tensor_copy(out=swo[:, q4 * 4 * D:(q4 + 1) * 4 * D],
                                                in_=swst[:]),
                 reads=[swst_b], writes=[swo_b], inc=self.dve)
        so4t = [self.sb(es, f"b3_o4t{i}", [128, D], BF16) for i in range(2)]
        sofm = [self.sb(es, f"b3_ofm{i}", [128, D], BF16) for i in range(2)]
        sxr = [self.sb(es, f"b3_x{i}", [128, 512], F32) for i in range(2)]
        so1, so1_b = self.sb(es, "b3_o1", [128, 512], F32)
        so2 = [self.sb(es, f"b3_o2{i}", [128, 512], F32) for i in range(2)]
        ptr = [self.ps(es, f"b3_ptr{i}", [128, D], BF16) for i in range(2)]
        pO = [self.ps(es, f"b3_pO{i}", [128, 512], F32) for i in range(2)]

        for tile in range(NT):
            par = tile % 2
            ot, ot_b = so4t[par]
            fm, fm_b = sofm[par]
            pt, pt_b = ptr[par]
            self.dma("sync", ot[:],
                     self.hbm["out4_tm"][tile * 128:(tile + 1) * 128, :],
                     writes=[ot_b], extra_wait=bar)
            def otr(t, ot=ot, pt=pt):
                ins = None
                for hc in range(KC):
                    ins = t.transpose(pt[:, hc * 128:(hc + 1) * 128],
                                      ot[:, hc * 128:(hc + 1) * 128], idt[:])
                return ins
            step(p, "tensor", otr, reads=[ot_b, idt_b], writes=[pt_b], inc=self.pe)
            step(p, "vector",
                 lambda v, fm=fm, pt=pt: v.tensor_copy(out=fm[:], in_=pt[:]),
                 reads=[pt_b], writes=[fm_b], inc=self.dve)
            for dg in range(4):
                pp, pp_b = pO[dg % 2]
                xb, xb_b = sxr[dg % 2]
                ob, ob_b = so2[dg % 2]
                def omm(t, dg=dg, fm=fm, pp=pp):
                    ins = None
                    for hc in range(KC):
                        ins = t.matmul(pp[:], fm[:, hc * 128:(hc + 1) * 128],
                                       swo[:, hc * D + dg * 512:hc * D + (dg + 1) * 512],
                                       start=(hc == 0), stop=(hc == KC - 1))
                    return ins
                step(p, "tensor", omm, reads=[fm_b, swo_b], writes=[pp_b], inc=self.pe)
                self.dma("vector", xb[:],
                         self.hbm["x2"][tile * 128:(tile + 1) * 128, dg * 512:(dg + 1) * 512],
                         writes=[xb_b], extra_wait=bar)
                step(p, "vector",
                     lambda v, dg=dg, pp=pp: v.scalar_tensor_tensor(
                         out=so1[:], in0=pp[:], scalar=1.0,
                         in1=sbot[:, dg * 512:(dg + 1) * 512],
                         op0=ALU.mult, op1=ALU.add),
                     reads=[pp_b, sbo_b], writes=[so1_b], inc=self.dve)
                step(p, "vector",
                     lambda v, ob=ob, xb=xb: v.tensor_add(out=ob[:], in0=so1[:], in1=xb[:]),
                     reads=[so1_b, xb_b], writes=[ob_b], inc=self.dve)
                self.dma("scalar",
                         self.hbm["x3"][tile * 128:(tile + 1) * 128, dg * 512:(dg + 1) * 512],
                         ob[:], reads=[ob_b])
        es.close()


def np_bf16(x):
    return np.asarray(x, dtype=ml_dtypes.bfloat16)


def make_inputs(core, inp):
    wqkvT = np.ascontiguousarray(inp["L0_attn_a_Wqkv"].T)
    sel = np.ascontiguousarray(
        wqkvT.reshape(D, 16, 384)[:, [core, core + 8], :].reshape(D, 768))
    maskM = np.zeros((128, 2048), np.float32)
    for p in range(128):
        maskM[p, p::128] = 1.0
    d = {
        "x_all": inp["L0_attn_a_query"],
        "x_own": np.ascontiguousarray(inp["L0_attn_a_query"][core]),
        "wqkvt_sel": sel,
        "wprojT": np.ascontiguousarray(inp["L0_attn_a_Wproj"].T),
        "bproj": inp["L0_attn_a_bproj"].reshape(1, D),
        "clsT": np.ascontiguousarray(inp["L0_ffn_classifier_w"].T),
        "corr": inp["L0_ffn_e_score_correction_bias"].reshape(1, 12),
        "gateT": np.ascontiguousarray(np.transpose(inp["L0_ffn_gate_w"], (0, 2, 1))),
        "upT": np.ascontiguousarray(np.transpose(inp["L0_ffn_up_w"], (0, 2, 1))),
        "downT": np.ascontiguousarray(np.transpose(inp["L0_ffn_down_w"], (0, 2, 1))),
        "wqT": np.ascontiguousarray(inp["L0_attn_b_wq"].T),
        "wkT": np.ascontiguousarray(inp["L0_attn_b_wk"].T),
        "wvT": np.ascontiguousarray(inp["L0_attn_b_wv"].T),
        "woT": np.ascontiguousarray(inp["L0_attn_b_wo"].T),
        "bq": inp["L0_attn_b_bq"].reshape(1, D),
        "bk": inp["L0_attn_b_bk"].reshape(1, D),
        "bv": inp["L0_attn_b_bv"].reshape(1, D),
        "bo": inp["L0_attn_b_bo"].reshape(1, D),
        "qnw": inp["L0_attn_b_q_norm_w"].reshape(1, D),
        "knw": inp["L0_attn_b_k_norm_w"].reshape(1, D),
        "mix": inp["L0_attn_b_score_mix_w"].reshape(1, 5),
        "ident_bf": np_bf16(np.eye(128, dtype=np.float32)),
        "ident_f": np.eye(128, dtype=np.float32),
        "iota12": np.broadcast_to(np.arange(12, dtype=np.float32), (128, 12)).copy(),
        "maskM": np_bf16(maskM),
        "ones1": np.ones((1, 128), np.float32),
    }
    return d


EMITTERS = {}


def build(phases):
    nc = bass.Bass("TRN2", target_bir_lowering=False)
    B = Bld(nc, phases)
    for ph in phases:
        EMITTERS[ph](B)
    end = B.barrier_toks()
    B.p.do("sync", lambda e: e.nop(), wait=end)
    B.p.realize(B.es)
    B.es.close()
    return nc, B


EMITTERS["A0"] = lambda B: B.emit_A0()
EMITTERS["A1"] = lambda B: B.emit_A1()
EMITTERS["A2"] = lambda B: B.emit_A2()
import os
EMITTERS["F0a"] = lambda B: B.emit_F0("x1", debug=bool(os.environ.get("MK_DEBUG")))
EMITTERS["F1a"] = lambda B: B.emit_F1()
EMITTERS["F2a"] = lambda B: B.emit_F2("x1", "x2")
EMITTERS["B0"] = lambda B: B.emit_B0()
EMITTERS["B1"] = lambda B: B.emit_B1()
EMITTERS["B2"] = lambda B: B.emit_B2()
EMITTERS["B3"] = lambda B: B.emit_B3()
EMITTERS["F0b"] = lambda B: B.emit_F0("x3")
EMITTERS["F1b"] = lambda B: B.emit_F1()
EMITTERS["F2b"] = lambda B: B.emit_F2("x3", "x4")


from concourse.bass_utils import run_bass_kernel_spmd


def kernel(**inputs):
    nc, B = build(ALL_PHASES)
    in_maps = []
    for core in range(8):
        m = {k: v for k, v in make_inputs(core, inputs).items() if k in B.used_inputs}
        in_maps.append(m)
    res = run_bass_kernel_spmd(nc, in_maps, core_ids=list(range(8)))
    out = np.stack([np.asarray(res.results[c]["x4"], np.float32) for c in range(8)], axis=0)
    return out


# version 03
# speedup vs baseline: 1.0150x; 1.0150x over previous
import contextlib


ENGS = ("tensor", "vector", "scalar", "gpsimd", "sync")


class Buf:
    __slots__ = ("w", "r", "ctr")

    def __init__(self):
        self.w = None
        self.r = []
        self.ctr = None

    def write_wait(self):
        return self.r + ([self.w] if self.w else [])

    def read_wait(self):
        return [self.w] if self.w else []


def step(p, eng, fn, reads=(), writes=(), inc=None, extra_wait=(), inc_k=1):
    wait = [t for t in extra_wait if t is not None]
    for b in reads:
        wait += b.read_wait()
    for b in writes:
        wait += b.write_wait()
    tok = p.do(eng, fn, wait=wait, inc=inc, inc_k=inc_k)
    for b in reads:
        b.r.append(tok)
    for b in writes:
        b.w = tok
        b.r = []
    return tok


class Ctr:
    __slots__ = ("name", "n", "sem")

    def __init__(self, name):
        self.name = name
        self.n = 0
        self.sem = None

    def tok(self):
        return (self, self.n)


class Prog:
    def __init__(self, nc):
        self.nc = nc
        self.steps = {e: [] for e in ENGS}
        self.ctrs = []

    def ctr(self, name):
        c = Ctr(name)
        self.ctrs.append(c)
        return c

    def do(self, eng, fn, wait=(), inc=None, inc_k=1):
        tok = None
        if inc is not None:
            inc.n += inc_k
            tok = (inc, inc.n)
        self.steps[eng].append((tuple(wait), fn, inc, inc_k))
        return tok

    def dma(self, eng, out, in_, wait=(), inc=None, **kw):
        def fn(e):
            return e.dma_start(out=out, in_=in_, **kw)
        return self.do(eng, fn, wait=wait, inc=inc, inc_k=16)

    def realize(self, es):
        for c in self.ctrs:
            c.sem = es.enter_context(self.nc.semaphore(c.name))
        block = es.enter_context(self.nc.Block())
        prog = self

        def emit(eng_name):
            def body(e):
                hw = {}
                for wait, fn, inc, inc_k in prog.steps[eng_name]:
                    merged = {}
                    for (c, v) in wait:
                        if v > merged.get(id(c), (None, 0))[1]:
                            merged[id(c)] = (c, v)
                    for (c, v) in merged.values():
                        if v <= 0 or hw.get(id(c), 0) >= v:
                            continue
                        e.wait_ge(c.sem, v)
                        hw[id(c)] = v
                    ins = fn(e)
                    if inc is not None:
                        if ins is None:
                            ins = e.nop()
                        ins.then_inc(inc.sem, inc_k)

            getattr(block, eng_name)(body)

        for eng_name in ENGS:
            if prog.steps[eng_name]:
                emit(eng_name)
        return block



import contextlib


ENGS = ("tensor", "vector", "scalar", "gpsimd", "sync")


class Buf:
    __slots__ = ("w", "r", "ctr")

    def __init__(self):
        self.w = None
        self.r = []
        self.ctr = None

    def write_wait(self):
        return self.r + ([self.w] if self.w else [])

    def read_wait(self):
        return [self.w] if self.w else []


def step(p, eng, fn, reads=(), writes=(), inc=None, extra_wait=(), inc_k=1):
    wait = [t for t in extra_wait if t is not None]
    for b in reads:
        wait += b.read_wait()
    for b in writes:
        wait += b.write_wait()
    tok = p.do(eng, fn, wait=wait, inc=inc, inc_k=inc_k)
    for b in reads:
        b.r.append(tok)
    for b in writes:
        b.w = tok
        b.r = []
    return tok


class Ctr:
    __slots__ = ("name", "n", "sem")

    def __init__(self, name):
        self.name = name
        self.n = 0
        self.sem = None

    def tok(self):
        return (self, self.n)


class Prog:
    def __init__(self, nc):
        self.nc = nc
        self.steps = {e: [] for e in ENGS}
        self.ctrs = []

    def ctr(self, name):
        c = Ctr(name)
        self.ctrs.append(c)
        return c

    def do(self, eng, fn, wait=(), inc=None, inc_k=1):
        tok = None
        if inc is not None:
            inc.n += inc_k
            tok = (inc, inc.n)
        self.steps[eng].append((tuple(wait), fn, inc, inc_k))
        return tok

    def dma(self, eng, out, in_, wait=(), inc=None, **kw):
        def fn(e):
            return e.dma_start(out=out, in_=in_, **kw)
        return self.do(eng, fn, wait=wait, inc=inc, inc_k=16)

    def realize(self, es):
        for c in self.ctrs:
            c.sem = es.enter_context(self.nc.semaphore(c.name))
        block = es.enter_context(self.nc.Block())
        prog = self

        def emit(eng_name):
            def body(e):
                hw = {}
                for wait, fn, inc, inc_k in prog.steps[eng_name]:
                    merged = {}
                    for (c, v) in wait:
                        if v > merged.get(id(c), (None, 0))[1]:
                            merged[id(c)] = (c, v)
                    for (c, v) in merged.values():
                        if v <= 0 or hw.get(id(c), 0) >= v:
                            continue
                        e.wait_ge(c.sem, v)
                        hw[id(c)] = v
                    ins = fn(e)
                    if inc is not None:
                        if ins is None:
                            ins = e.nop()
                        ins.then_inc(inc.sem, inc_k)

            getattr(block, eng_name)(body)

        for eng_name in ENGS:
            if prog.steps[eng_name]:
                emit(eng_name)
        return block


import contextlib
import numpy as np
import ml_dtypes
import concourse.bass as bass
import concourse.mybir as mybir

F32 = mybir.dt.float32
BF16 = mybir.dt.bfloat16
AF = mybir.ActivationFunctionType
ALU = mybir.AluOpType
AX = mybir.AxisListType

L = 4096
D = 2048
NT = L // 128
KC = D // 128
INV_SQRT_HD = 0.08838834764831845
SEQ_SHIFTS = [0, 1, -1, 3, -3, 9, -9, 26, -26, 78, -78, 232, -232, 689, -689, 2048]
CH_SHIFTS = [1, 2, 4, 8]

ALL_PHASES = ["A0", "A1", "A2", "F0a", "F1a", "F2a", "B0", "B1", "B2", "B3",
              "F0b", "F1b", "F2b"]

SCRATCH = {
    "ln_fm_all": ([8, 128, KC, L], BF16, ["A0"], ["A1"]),
    "ctx_fm":    ([16, 128, L], BF16, ["A1"], ["A2"]),
    "x1":        ([L, D], F32, ["A2"], ["F0a", "F2a"]),
    "x2":        ([L, D], F32, ["F2a"], ["B0", "B3"]),
    "x3":        ([L, D], F32, ["B3"], ["F0b", "F2b"]),
    "ln_fm":     ([128, KC, L], BF16, ["F0a", "B0", "F0b"], ["F1a", "B1", "F1b"]),
    "ln_tm":     ([L, D], BF16, ["F0a", "F0b"], ["F2a", "F2b"]),
    "wfullT_h":  ([12, L], F32, ["F0a", "F0b"], ["F1a", "F1b"]),
    "z_tm":      ([128, NT], F32, ["F0a", "F0b"], ["F2a", "F2b"]),
    "actw_t":    ([NT, 128, 8, 8, 128], BF16, ["F1a", "F1b"], ["F2a", "F2b"]),
    "qmix_tm":   ([L, D], BF16, ["B1"], ["B2"]),
    "kn_tm":     ([L, D], BF16, ["B1"], ["B2"]),
    "v_tm":      ([L, D], BF16, ["B1"], ["B2"]),
    "out4_tm":   ([L, D], BF16, ["B2"], ["B3"]),
}

INPUTS = {
    "x_all":     ([8, L, D], F32),
    "x_own":     ([L, D], F32),
    "wqkvt_sel": ([D, 768], F32),
    "wprojT":    ([D, D], F32),
    "bproj":     ([1, D], F32),
    "clsT":      ([D, 12], F32),
    "corr":      ([1, 12], F32),
    "gateT":     ([8, D, 1024], F32),
    "upT":       ([8, D, 1024], F32),
    "downT":     ([8, 1024, D], F32),
    "wqT":       ([D, D], F32),
    "wkT":       ([D, D], F32),
    "wvT":       ([D, D], F32),
    "woT":       ([D, D], F32),
    "bq":        ([1, D], F32),
    "bk":        ([1, D], F32),
    "bv":        ([1, D], F32),
    "bo":        ([1, D], F32),
    "qnw":       ([1, D], F32),
    "knw":       ([1, D], F32),
    "mix":       ([1, 5], F32),
    "ident_bf":  ([128, 128], BF16),
    "ident_f":   ([128, 128], F32),
    "iota12":    ([128, 12], F32),
    "maskM":     ([128, 2048], BF16),
    "ones1":     ([1, 128], F32),
}

PHASE_INPUTS = {
    "A0": ["x_all", "ident_bf"],
    "A1": ["wqkvt_sel", "ident_bf"],
    "A2": ["wprojT", "bproj", "x_own", "ones1"],
    "F0": ["clsT", "corr", "iota12", "ones1", "ident_bf", "ident_f"],
    "F1": ["gateT", "upT", "ones1"],
    "F2": ["downT"],
    "B0": ["ident_bf"],
    "B1": ["wqT", "wkT", "wvT", "bq", "bk", "bv", "qnw", "knw", "mix", "ones1"],
    "B2": ["maskM", "ident_bf"],
    "B3": ["woT", "bo", "ones1", "ident_bf"],
}


class Bld:
    def __init__(self, nc, phases):
        self.nc = nc
        self.phases = list(phases)
        self.p = Prog(nc)
        self.es = contextlib.ExitStack()
        self.pe = self.p.ctr("pe")
        self.dve = self.p.ctr("dve")
        self.act = self.p.ctr("act")
        self.gp = self.p.ctr("gp")
        self.qs = self.p.ctr("qs")
        self.qc = self.p.ctr("qc")
        self.qv = self.p.ctr("qv")
        self.qg = self.p.ctr("qg")
        self.hbm = {}
        self._nbufctr = 0
        self._nm = 0
        self._ctr_pool = []
        self._phase_ctrs = []
        self.store = self._buf_ctr()
        self.out_toks = []
        self._declare_hbm()
        self.ident = None

    def _buf_ctr(self):
        if self._ctr_pool:
            c = self._ctr_pool.pop()
        else:
            self._nbufctr += 1
            c = self.p.ctr(f"bq{self._nbufctr}")
        self._phase_ctrs.append(c)
        return c

    def _needs_input(self, name):
        for ph in self.phases:
            key = ph[:-1] if (ph not in PHASE_INPUTS and ph[:-1] in PHASE_INPUTS) else ph
            if name in PHASE_INPUTS.get(key, []):
                return True
        return False

    def _declare_hbm(self):
        nc = self.nc
        fused = set(self.phases) == set(ALL_PHASES)
        self.used_inputs = []
        for name, (shape, dt) in INPUTS.items():
            if fused or self._needs_input(name):
                self.hbm[name] = nc.dram_tensor(name, shape, dt, kind="ExternalInput").ap()
                self.used_inputs.append(name)
        self.scratch_kinds = {}
        for name, (shape, dt, prods, cons) in SCRATCH.items():
            produced = any(p in self.phases for p in prods)
            consumed = any(c in self.phases for c in cons)
            if fused:
                kind = "Internal"
            elif produced:
                kind = "ExternalOutput"
            elif consumed:
                kind = "ExternalInput"
            else:
                continue
            self.scratch_kinds[name] = kind
            self.hbm[name] = nc.dram_tensor(name, shape, dt, kind=kind).ap()
        if fused or "F2b" in self.phases:
            self.hbm["x4"] = nc.dram_tensor("x4", [L, D], F32, kind="ExternalOutput").ap()

    def sb(self, es, name, shape, dt=F32):
        self._nm += 1
        t = es.enter_context(self.nc.sbuf_tensor(f"{name}_{self._nm}", shape, dt))
        return t, Buf()

    def ps(self, es, name, shape, dt=F32):
        self._nm += 1
        t = es.enter_context(self.nc.psum_tensor(f"{name}_{self._nm}", shape, dt))
        return t, Buf()

    def barrier_toks(self):
        toks = [c.tok() for c in self.p.ctrs]
        self._ctr_pool.extend(self._phase_ctrs)
        self._phase_ctrs = []
        if self.ident is not None:
            self.ident[1].w = None
            self.ident[1].r = []
            self.ident[1].ctr = None
        self.store = self._buf_ctr()
        return toks

    def dma(self, qname, out, in_, reads=(), writes=(), extra_wait=()):
        if qname == "vector":
            qname = "gpsimd"
        if len(writes) == 1:
            b = writes[0]
        elif len(reads) >= 1:
            b = reads[0]
        else:
            b = None
        if b is not None:
            if b.ctr is None:
                b.ctr = self._buf_ctr()
            ctr = b.ctr
        else:
            ctr = self.store
        return step(self.p, qname, lambda e, out=out, in_=in_: e.dma_start(out=out, in_=in_),
                    reads=reads, writes=writes, inc=ctr, inc_k=16,
                    extra_wait=extra_wait)

    def load_ident(self, es, bar):
        if self.ident is None:
            self._nm += 1
            t = self.es.enter_context(
                self.nc.sbuf_tensor(f"g_ident_{self._nm}", [128, 128], BF16, side="right"))
            b = Buf()
            self.dma("sync", t[:], self.hbm["ident_bf"], writes=[b], extra_wait=bar)
            self.ident = (t, b)
        return self.ident

    def row_tile(self, es, bar, name, src_row_ap, width, out_dt=F32, scale=None, ps=None):
        p = self.p
        srow, srow_b = self.sb(es, name + "_row", [1, width], F32)
        stile, stile_b = self.sb(es, name + "_tile", [128, width], out_dt)
        if ps is None:
            prow, prow_b = self.ps(es, name + "_ps", [128, 512], F32)
        else:
            prow, prow_b = ps
        sones, sones_b = self.sb(es, name + "_ones", [1, 128], F32)
        self.dma("sync", sones[:], self.hbm["ones1"], writes=[sones_b], extra_wait=bar)
        self.dma("sync", srow[:], src_row_ap, writes=[srow_b], extra_wait=bar)
        for j0 in range(0, width, 512):
            w = min(512, width - j0)
            step(p, "tensor",
                 lambda t, j0=j0, w=w: t.matmul(prow[:, 0:w], sones[:], srow[:, j0:j0 + w],
                                                start=True, stop=True),
                 reads=[sones_b, srow_b], writes=[prow_b], inc=self.pe)
            if scale is None:
                step(p, "vector",
                     lambda v, j0=j0, w=w: v.tensor_copy(out=stile[:, j0:j0 + w],
                                                         in_=prow[:, 0:w]),
                     reads=[prow_b], writes=[stile_b], inc=self.dve)
            else:
                step(p, "vector",
                     lambda v, j0=j0, w=w: v.tensor_scalar_mul(out=stile[:, j0:j0 + w],
                                                               in0=prow[:, 0:w], scalar1=scale),
                     reads=[prow_b], writes=[stile_b], inc=self.dve)
        return stile, stile_b

    def emit_ln(self, es, bar, tag, jobs, tile_hook=None):
        p = self.p
        sx, sx_b, sln, sln_b, sfm, sfm_b, ptr, ptr_b = [], [], [], [], [], [], [], []
        for par in range(2):
            t, b = self.sb(es, f"{tag}_x{par}", [128, D], F32); sx.append(t); sx_b.append(b)
            t, b = self.sb(es, f"{tag}_ln{par}", [128, D], BF16); sln.append(t); sln_b.append(b)
            t, b = self.sb(es, f"{tag}_fm{par}", [128, D], BF16); sfm.append(t); sfm_b.append(b)
            t, b = self.ps(es, f"{tag}_ptr{par}", [128, D], BF16); ptr.append(t); ptr_b.append(b)
        sjunk, sjunk_b = self.sb(es, f"{tag}_junk", [128, D], BF16)
        ssum, ssum_b = self.sb(es, f"{tag}_sum", [128, 2], F32)
        sst, sst_b = self.sb(es, f"{tag}_st", [128, 8], F32)
        idt, idt_b = self.load_ident(es, bar)

        for src2d, dst_fm, dst_tm in jobs:
            for T in range(NT):
                par = T % 2
                self.dma("sync", sx[par][:], src2d[T * 128:(T + 1) * 128, :],
                         writes=[sx_b[par]], extra_wait=bar)
                step(p, "vector",
                     lambda v, par=par: v.reduce_sum(out=ssum[:, 0:1], in_=sx[par][:],
                                                     axis=AX.X),
                     reads=[sx_b[par]], writes=[ssum_b], inc=self.dve)
                step(p, "vector",
                     lambda v, par=par: v.scalar_tensor_tensor(
                         out=sjunk[:], in0=sx[par][:], scalar=1.0, in1=sx[par][:],
                         op0=ALU.mult, op1=ALU.mult, accum_out=ssum[:, 1:2]),
                     reads=[sx_b[par]], writes=[sjunk_b, ssum_b], inc=self.dve)
                step(p, "vector",
                     lambda v: v.tensor_scalar_mul(out=sst[:, 0:2], in0=ssum[:, 0:2],
                                                   scalar1=1.0 / D),
                     reads=[ssum_b], writes=[sst_b], inc=self.dve)
                step(p, "vector",
                     lambda v: v.tensor_mul(out=sst[:, 2:3], in0=sst[:, 0:1], in1=sst[:, 0:1]),
                     reads=[sst_b], writes=[sst_b], inc=self.dve)
                step(p, "vector",
                     lambda v: v.tensor_sub(out=sst[:, 3:4], in0=sst[:, 1:2], in1=sst[:, 2:3]),
                     reads=[sst_b], writes=[sst_b], inc=self.dve)
                step(p, "vector",
                     lambda v: v.tensor_scalar_add(out=sst[:, 7:8], in0=sst[:, 3:4],
                                                   scalar1=1e-5),
                     reads=[sst_b], writes=[sst_b], inc=self.dve)
                step(p, "scalar",
                     lambda s: s.activation(out=sst[:, 4:5], in_=sst[:, 7:8], func=AF.Sqrt,
                                            bias=0.0, scale=1.0),
                     reads=[sst_b], writes=[sst_b], inc=self.act)
                step(p, "vector",
                     lambda v: v.reciprocal(out=sst[:, 5:6], in_=sst[:, 4:5]),
                     reads=[sst_b], writes=[sst_b], inc=self.dve)
                step(p, "vector",
                     lambda v: v.scalar_tensor_tensor(out=sst[:, 6:7], in0=sst[:, 0:1],
                                                      scalar=-1.0, in1=sst[:, 5:6],
                                                      op0=ALU.mult, op1=ALU.mult),
                     reads=[sst_b], writes=[sst_b], inc=self.dve)
                step(p, "scalar",
                     lambda s, par=par: s.activation(out=sln[par][:], in_=sx[par][:],
                                                     func=AF.Identity, bias=sst[:, 6:7],
                                                     scale=sst[:, 5:6]),
                     reads=[sx_b[par], sst_b], writes=[sln_b[par]], inc=self.act)
                if dst_tm is not None:
                    self.dma("vector", dst_tm[T * 128:(T + 1) * 128, :], sln[par][:],
                             reads=[sln_b[par]])
                if dst_fm is not None:
                    def tr(t, par=par):
                        ins = None
                        for hc in range(KC):
                            ins = t.transpose(ptr[par][:, hc * 128:(hc + 1) * 128],
                                              sln[par][:, hc * 128:(hc + 1) * 128], idt[:])
                        return ins
                    step(p, "tensor", tr, reads=[sln_b[par], idt_b], writes=[ptr_b[par]],
                         inc=self.pe)
                    step(p, "vector",
                         lambda v, par=par: v.tensor_copy(out=sfm[par][:], in_=ptr[par][:]),
                         reads=[ptr_b[par]], writes=[sfm_b[par]], inc=self.dve)
                    for q in range(4):
                        src3 = sfm[par][:, q * 512:(q + 1) * 512].rearrange(
                            "p (hc t) -> p hc t", t=128)
                        dst3 = dst_fm[:, q * 4:(q + 1) * 4, T * 128:(T + 1) * 128]
                        self.dma("scalar", dst3, src3, reads=[sfm_b[par]])
                    if tile_hook is not None:
                        tile_hook(T, sfm[par], sfm_b[par])

    def emit_A1(self):
        es = contextlib.ExitStack()
        bar = self.barrier_toks()
        p = self.p
        idt, idt_b = self.load_ident(es, bar)

        swst, swst_b = self.sb(es, "a1_wst", [128, 4 * 768], F32)
        swsel, swsel_b = self.sb(es, "a1_wsel", [128, KC * 768], BF16)
        wsrc = self.hbm["wqkvt_sel"].rearrange("(kc p) c -> p kc c", p=128)
        for q4 in range(4):
            self.dma("sync", swst[:].rearrange("p (kc c) -> p kc c", c=768),
                     wsrc[:, q4 * 4:(q4 + 1) * 4, :], writes=[swst_b], extra_wait=bar)
            step(p, "vector",
                 lambda v, q4=q4: v.tensor_copy(out=swsel[:, q4 * 4 * 768:(q4 + 1) * 4 * 768],
                                                in_=swst[:]),
                 reads=[swst_b], writes=[swsel_b], inc=self.dve)

        sln = []
        for par in range(2):
            sln.append(self.sb(es, f"a1_lnr{par}", [128, KC * 512], BF16))
        sqT = [self.sb(es, f"a1_qT{sl}", [128, L], BF16) for sl in range(2)]
        skT = [self.sb(es, f"a1_kT{sl}", [128, L], BF16) for sl in range(2)]
        svtm = [self.sb(es, f"a1_vtm{sl}", [128, L], BF16) for sl in range(2)]
        svfm, svfm_b = self.sb(es, "a1_vfm", [128, L], BF16)
        sp_, sp_b = self.sb(es, "a1_p", [128, 4 * L], BF16)
        spT, spT_b = self.sb(es, "a1_pT", [128, 32 * 512], BF16)
        sls, sls_b = self.sb(es, "a1_ls", [128, 16], F32)
        srec, srec_b = self.sb(es, "a1_rec", [128, 8], F32)
        sctxe, sctxe_b = self.sb(es, "a1_ctxe", [128, 512], BF16)

        pA = self.ps(es, "a1_pA", [128, 512], F32)
        pB = self.ps(es, "a1_pB", [128, 512], F32)
        pS = [self.ps(es, f"a1_pS{i}", [128, 1024], F32) for i in range(2)]
        ppT = [self.ps(es, f"a1_ppT{i}", [128, 1024], BF16) for i in range(2)]

        for m in range(8):
            for grp in range(8):
                par = grp % 2
                lnb, lnb_b = sln[par]
                src = self.hbm["ln_fm_all"][m][:, :, grp * 512:(grp + 1) * 512]
                for h8 in range(2):
                    self.dma("sync",
                             lnb[:, h8 * 8 * 512:(h8 + 1) * 8 * 512].rearrange(
                                 "p (kc t) -> p kc t", t=512),
                             src[:, h8 * 8:(h8 + 1) * 8, :],
                             writes=[lnb_b], extra_wait=bar)
                for sl in range(2):
                    for kind in range(3):
                        pp, pp_b = (pA, pB)[(sl * 3 + kind) % 2]
                        def qkv_mm(t, sl=sl, kind=kind, par=par, pp=pp):
                            lnt, _ = sln[par]
                            ins = None
                            for kc in range(KC):
                                ins = t.matmul(
                                    pp[:],
                                    swsel[:, kc * 768 + sl * 384 + kind * 128:
                                          kc * 768 + sl * 384 + (kind + 1) * 128],
                                    lnt[:, kc * 512:(kc + 1) * 512],
                                    start=(kc == 0), stop=(kc == KC - 1))
                            return ins
                        t_mm = step(p, "tensor", qkv_mm, reads=[swsel_b, lnb_b],
                                    writes=[pp_b], inc=self.pe)
                        dstmap = [sqT[sl], skT[sl], (svfm, svfm_b)]
                        dst, dst_b = dstmap[kind]
                        step(p, "vector",
                             lambda v, dst=dst, grp=grp, pp=pp:
                                 v.tensor_copy(out=dst[:, grp * 512:(grp + 1) * 512],
                                               in_=pp[:]),
                             reads=[pp_b], writes=[dst_b], inc=self.dve)
                    def vtr(t, sl=sl, grp=grp):
                        pt, _ = ppT[grp % 2]
                        ins = None
                        for j in range(4):
                            ins = t.transpose(pt[:, j * 128:(j + 1) * 128],
                                              svfm[:, (grp * 4 + j) * 128:
                                                   (grp * 4 + j + 1) * 128], idt[:])
                        return ins
                    ptb, ptb_b = ppT[grp % 2]
                    step(p, "tensor", vtr, reads=[svfm_b, idt_b], writes=[ptb_b],
                         inc=self.pe)
                    step(p, "vector",
                         lambda v, sl=sl, grp=grp, ptb=ptb:
                             v.tensor_copy(out=svtm[sl][0][:, grp * 512:(grp + 1) * 512],
                                           in_=ptb[:, 0:512]),
                         reads=[ptb_b], writes=[svtm[sl][1]], inc=self.dve)

            for sl in range(2):
                slot = 2 * m + sl
                qTt, qT_b = sqT[sl]
                kTt, kT_b = skT[sl]
                vt, vt_b = svtm[sl]
                for qg in range(8):
                    for ti in range(4):
                        qt0 = qg * 512 + ti * 128
                        for qu in range(4):
                            psq, psq_b = pS[qu % 2]
                            def smm(t, qt0=qt0, qu=qu, psq=psq, qTt=qTt, kTt=kTt):
                                ins = None
                                for h in range(2):
                                    kt0 = qu * 1024 + h * 512
                                    ins = t.matmul(psq[:, h * 512:(h + 1) * 512],
                                                   qTt[:, qt0:qt0 + 128],
                                                   kTt[:, kt0:kt0 + 512],
                                                   start=True, stop=True)
                                return ins
                            step(p, "tensor", smm, reads=[qT_b, kT_b], writes=[psq_b],
                                 inc=self.pe)
                            step(p, "scalar",
                                 lambda s, ti=ti, qu=qu, psq=psq:
                                     s.activation(out=sp_[:, ti * L + qu * 1024:
                                                          ti * L + (qu + 1) * 1024],
                                                  in_=psq[:], func=AF.Exp, bias=0.0,
                                                  scale=INV_SQRT_HD,
                                                  accum_out=sls[:, ti * 4 + qu:
                                                               ti * 4 + qu + 1]),
                                 reads=[psq_b], writes=[sp_b, sls_b], inc=self.act)
                        step(p, "vector",
                             lambda v, ti=ti: v.reduce_sum(out=srec[:, ti:ti + 1],
                                                           in_=sls[:, ti * 4:(ti + 1) * 4],
                                                           axis=AX.X),
                             reads=[sls_b], writes=[srec_b], inc=self.dve)
                        step(p, "vector",
                             lambda v, ti=ti: v.reciprocal(out=srec[:, 4 + ti:5 + ti],
                                                           in_=srec[:, ti:ti + 1]),
                             reads=[srec_b], writes=[srec_b], inc=self.dve)
                        step(p, "vector",
                             lambda v, ti=ti: v.tensor_scalar(
                                 out=sp_[:, ti * L:(ti + 1) * L],
                                 in0=sp_[:, ti * L:(ti + 1) * L],
                                 scalar1=srec[:, 4 + ti:5 + ti], scalar2=None,
                                 op0=ALU.mult),
                             reads=[srec_b], writes=[sp_b], inc=self.dve)
                    for f in range(16):
                        ptb, ptb_b = ppT[f % 2]
                        def ptr_f(t, f=f, ptb=ptb):
                            ins = None
                            for j in range(8):
                                kc = 2 * f + j // 4
                                ti = j % 4
                                ins = t.transpose(ptb[:, j * 128:(j + 1) * 128],
                                                  sp_[:, ti * L + kc * 128:
                                                       ti * L + (kc + 1) * 128], idt[:])
                            return ins
                        step(p, "tensor", ptr_f, reads=[sp_b, idt_b], writes=[ptb_b],
                             inc=self.pe)
                        step(p, "vector",
                             lambda v, f=f, ptb=ptb: v.tensor_copy(
                                 out=spT[:, f * 1024:(f + 1) * 1024], in_=ptb[:]),
                             reads=[ptb_b], writes=[spT_b], inc=self.dve)
                    pc, pc_b = (pA, pB)[qg % 2]
                    def cmm(t, vt=vt, pc=pc):
                        ins = None
                        for kc in range(32):
                            ins = t.matmul(pc[:], vt[:, kc * 128:(kc + 1) * 128],
                                           spT[:, kc * 512:(kc + 1) * 512],
                                           start=(kc == 0), stop=(kc == 31))
                        return ins
                    step(p, "tensor", cmm, reads=[vt_b, spT_b], writes=[pc_b], inc=self.pe)
                    step(p, "vector",
                         lambda v, pc=pc: v.tensor_copy(out=sctxe[:], in_=pc[:]),
                         reads=[pc_b], writes=[sctxe_b], inc=self.dve)
                    self.dma("scalar",
                             self.hbm["ctx_fm"][slot][:, qg * 512:(qg + 1) * 512],
                             sctxe[:], reads=[sctxe_b])
        es.close()

    def emit_A2(self):
        es = contextlib.ExitStack()
        bar = self.barrier_toks()
        p = self.p
        swst, swst_b = self.sb(es, "a2_wst", [128, 4 * D], F32)
        swp, swp_b = self.sb(es, "a2_wp", [128, KC * D], BF16)
        wsrc = self.hbm["wprojT"].rearrange("(hc p) do -> p hc do", p=128)
        for q4 in range(4):
            self.dma("sync", swst[:].rearrange("p (hc do) -> p hc do", do=D),
                     wsrc[:, q4 * 4:(q4 + 1) * 4, :], writes=[swst_b], extra_wait=bar)
            step(p, "vector",
                 lambda v, q4=q4: v.tensor_copy(out=swp[:, q4 * 4 * D:(q4 + 1) * 4 * D],
                                                in_=swst[:]),
                 reads=[swst_b], writes=[swp_b], inc=self.dve)
        sbt, sbt_b = self.row_tile(es, bar, "a2_bp", self.hbm["bproj"], D)
        scg = [self.sb(es, f"a2_cg{i}", [128, 16 * 512], BF16) for i in range(2)]
        sxr = [self.sb(es, f"a2_x{i}", [128, 512], F32) for i in range(2)]
        so1, so1_b = self.sb(es, "a2_o1", [128, 512], F32)
        so2 = [self.sb(es, f"a2_o2{i}", [128, 512], F32) for i in range(2)]
        pO = [self.ps(es, f"a2_pO{i}", [128, 512], F32) for i in range(2)]
        ctx = self.hbm["ctx_fm"]
        for grp in range(8):
            cgt, cg_b = scg[grp % 2]
            self.dma("sync", cgt[:].rearrange("p (s t) -> p s t", t=512),
                     ctx.rearrange("s p t -> p s t")[:, :, grp * 512:(grp + 1) * 512],
                     writes=[cg_b], extra_wait=bar)
            for ti in range(4):
                row0 = grp * 512 + ti * 128
                for dg in range(4):
                    pp, pp_b = pO[dg % 2]
                    xb, xb_b = sxr[dg % 2]
                    ob, ob_b = so2[dg % 2]
                    def omm(t, ti=ti, dg=dg, cgt=cgt, pp=pp):
                        ins = None
                        for hc in range(KC):
                            ins = t.matmul(pp[:],
                                           cgt[:, hc * 512 + ti * 128:hc * 512 + (ti + 1) * 128],
                                           swp[:, hc * D + dg * 512:hc * D + (dg + 1) * 512],
                                           start=(hc == 0), stop=(hc == KC - 1))
                        return ins
                    step(p, "tensor", omm, reads=[cg_b, swp_b], writes=[pp_b], inc=self.pe)
                    self.dma("vector", xb[:],
                             self.hbm["x_own"][row0:row0 + 128, dg * 512:(dg + 1) * 512],
                             writes=[xb_b], extra_wait=bar)
                    step(p, "vector",
                         lambda v, dg=dg, pp=pp: v.scalar_tensor_tensor(
                             out=so1[:], in0=pp[:], scalar=1.0,
                             in1=sbt[:, dg * 512:(dg + 1) * 512],
                             op0=ALU.mult, op1=ALU.add),
                         reads=[pp_b, sbt_b], writes=[so1_b], inc=self.dve)
                    step(p, "vector",
                         lambda v, ob=ob, xb=xb: v.tensor_add(out=ob[:], in0=so1[:], in1=xb[:]),
                         reads=[so1_b, xb_b], writes=[ob_b], inc=self.dve)
                    self.dma("scalar",
                             self.hbm["x1"][row0:row0 + 128, dg * 512:(dg + 1) * 512],
                             ob[:], reads=[ob_b])
        es.close()

    def emit_F0(self, xin, debug=False):
        es = contextlib.ExitStack()
        bar = self.barrier_toks()
        p = self.p
        if debug:
            dbg_lg = self.nc.dram_tensor("dbg_lg", [L, 12], F32, kind="ExternalOutput").ap()
            dbg_pb = self.nc.dram_tensor("dbg_pb", [L, 12], F32, kind="ExternalOutput").ap()
            dbg_bi = self.nc.dram_tensor("dbg_bi", [L, 12], F32, kind="ExternalOutput").ap()
            dbg_m = self.nc.dram_tensor("dbg_m", [L, 24], F32, kind="ExternalOutput").ap()
        sidf, sidf_b = self.sb(es, "f0_idf", [128, 128], F32)
        self.dma("sync", sidf[:], self.hbm["ident_f"], writes=[sidf_b], extra_wait=bar)
        sct, sct_b = self.sb(es, "f0_clsT", [128, KC * 12], BF16)
        scts, scts_b = self.sb(es, "f0_clsTs", [128, KC * 12], F32)
        self.dma("sync", scts[:].rearrange("p (hc e) -> p hc e", e=12),
                 self.hbm["clsT"].rearrange("(hc p) e -> p hc e", p=128),
                 writes=[scts_b], extra_wait=bar)
        step(p, "vector", lambda v: v.tensor_copy(out=sct[:], in_=scts[:]),
             reads=[scts_b], writes=[sct_b], inc=self.dve)
        scorr, scorr_b = self.row_tile(es, bar, "f0_corr", self.hbm["corr"], 12)
        siota, siota_b = self.sb(es, "f0_iota", [128, 12], F32)
        self.dma("sync", siota[:], self.hbm["iota12"], writes=[siota_b], extra_wait=bar)
        slg, slg_b = self.sb(es, "f0_lg", [128, 12], F32)
        srw, srw_b = self.sb(es, "f0_rw", [128, 80], F32)
        swf, swf_b = self.sb(es, "f0_wf", [128, 12], F32)
        swft, swft_b = self.sb(es, "f0_wft", [12, 128], F32)
        szacc, szacc_b = self.sb(es, "f0_zacc", [128, NT], F32)
        pR = self.ps(es, "f0_pR", [128, 12], F32)
        pT12 = self.ps(es, "f0_pT12", [12, 128], F32)

        def router_hook(T, sfmt, sfm_b):
            pRt, pR_b = pR
            def lmm(t, sfmt=sfmt):
                ins = None
                for hc in range(KC):
                    ins = t.matmul(pRt[:], sfmt[:, hc * 128:(hc + 1) * 128],
                                   sct[:, hc * 12:(hc + 1) * 12],
                                   start=(hc == 0), stop=(hc == KC - 1))
                return ins
            step(p, "tensor", lmm, reads=[sfm_b, sct_b], writes=[pR_b], inc=self.pe)
            step(p, "vector", lambda v: v.tensor_copy(out=slg[:], in_=pRt[:]),
                 reads=[pR_b], writes=[slg_b], inc=self.dve)
            step(p, "vector", lambda v: v.reduce_max(out=srw[:, 0:1], in_=slg[:], axis=AX.X),
                 reads=[slg_b], writes=[srw_b], inc=self.dve)
            step(p, "vector", lambda v: v.tensor_scalar_mul(out=srw[:, 1:2], in0=srw[:, 0:1],
                                                            scalar1=-1.0),
                 reads=[srw_b], writes=[srw_b], inc=self.dve)
            step(p, "scalar", lambda s: s.activation(out=srw[:, 16:28], in_=slg[:], func=AF.Exp,
                                                     bias=srw[:, 1:2], scale=1.0,
                                                     accum_out=srw[:, 2:3]),
                 reads=[slg_b, srw_b], writes=[srw_b], inc=self.act)
            step(p, "vector", lambda v: v.reciprocal(out=srw[:, 3:4], in_=srw[:, 2:3]),
                 reads=[srw_b], writes=[srw_b], inc=self.dve)
            step(p, "vector", lambda v: v.tensor_scalar(out=srw[:, 32:44], in0=srw[:, 16:28],
                                                        scalar1=srw[:, 3:4], scalar2=None,
                                                        op0=ALU.mult),
                 reads=[srw_b], writes=[srw_b], inc=self.dve)
            step(p, "vector", lambda v: v.tensor_add(out=srw[:, 44:56], in0=srw[:, 32:44],
                                                     in1=scorr[:, 0:12]),
                 reads=[srw_b, scorr_b], writes=[srw_b], inc=self.dve)
            if debug:
                self.dma("scalar", dbg_pb[T * 128:(T + 1) * 128, :], srw[:, 32:44],
                         reads=[srw_b])
                self.dma("scalar", dbg_bi[T * 128:(T + 1) * 128, :], srw[:, 44:56],
                         reads=[srw_b])
            step(p, "vector", lambda v: v.reduce_max(out=srw[:, 4:5], in_=srw[:, 44:56],
                                                     axis=AX.X),
                 reads=[srw_b], writes=[srw_b], inc=self.dve)
            step(p, "vector", lambda v: v.tensor_scalar(out=srw[:, 16:28], in0=srw[:, 44:56],
                                                        scalar1=srw[:, 4:5], scalar2=None,
                                                        op0=ALU.is_ge),
                 reads=[srw_b], writes=[srw_b], inc=self.dve)
            step(p, "vector", lambda v: v.scalar_tensor_tensor(out=srw[:, 16:28],
                                                               in0=srw[:, 16:28], scalar=-1e4,
                                                               in1=siota[:], op0=ALU.mult,
                                                               op1=ALU.add),
                 reads=[srw_b, siota_b], writes=[srw_b], inc=self.dve)
            step(p, "vector", lambda v: v.tensor_scalar_add(out=srw[:, 16:28],
                                                            in0=srw[:, 16:28], scalar1=1e4),
                 reads=[srw_b], writes=[srw_b], inc=self.dve)
            step(p, "vector", lambda v: v.tensor_reduce(out=srw[:, 5:6], in_=srw[:, 16:28],
                                                        axis=AX.X, op=ALU.min),
                 reads=[srw_b], writes=[srw_b], inc=self.dve)
            step(p, "vector", lambda v: v.tensor_scalar(out=srw[:, 16:28], in0=siota[:],
                                                        scalar1=srw[:, 5:6], scalar2=None,
                                                        op0=ALU.is_equal),
                 reads=[srw_b, siota_b], writes=[srw_b], inc=self.dve)
            step(p, "vector", lambda v: v.scalar_tensor_tensor(out=srw[:, 56:68],
                                                               in0=srw[:, 16:28], scalar=-1e30,
                                                               in1=srw[:, 44:56], op0=ALU.mult,
                                                               op1=ALU.add),
                 reads=[srw_b], writes=[srw_b], inc=self.dve)
            step(p, "vector", lambda v: v.tensor_copy(out=srw[:, 68:80], in_=srw[:, 16:28]),
                 reads=[srw_b], writes=[srw_b], inc=self.dve)
            step(p, "vector", lambda v: v.reduce_max(out=srw[:, 6:7], in_=srw[:, 56:68],
                                                     axis=AX.X),
                 reads=[srw_b], writes=[srw_b], inc=self.dve)
            step(p, "vector", lambda v: v.tensor_scalar(out=srw[:, 16:28], in0=srw[:, 56:68],
                                                        scalar1=srw[:, 6:7], scalar2=None,
                                                        op0=ALU.is_ge),
                 reads=[srw_b], writes=[srw_b], inc=self.dve)
            step(p, "vector", lambda v: v.scalar_tensor_tensor(out=srw[:, 16:28],
                                                               in0=srw[:, 16:28], scalar=-1e4,
                                                               in1=siota[:], op0=ALU.mult,
                                                               op1=ALU.add),
                 reads=[srw_b, siota_b], writes=[srw_b], inc=self.dve)
            step(p, "vector", lambda v: v.tensor_scalar_add(out=srw[:, 16:28],
                                                            in0=srw[:, 16:28], scalar1=1e4),
                 reads=[srw_b], writes=[srw_b], inc=self.dve)
            step(p, "vector", lambda v: v.tensor_reduce(out=srw[:, 7:8], in_=srw[:, 16:28],
                                                        axis=AX.X, op=ALU.min),
                 reads=[srw_b], writes=[srw_b], inc=self.dve)
            step(p, "vector", lambda v: v.tensor_scalar(out=srw[:, 16:28], in0=siota[:],
                                                        scalar1=srw[:, 7:8], scalar2=None,
                                                        op0=ALU.is_equal),
                 reads=[srw_b, siota_b], writes=[srw_b], inc=self.dve)
            step(p, "vector", lambda v: v.tensor_add(out=srw[:, 16:28], in0=srw[:, 16:28],
                                                     in1=srw[:, 68:80]),
                 reads=[srw_b], writes=[srw_b], inc=self.dve)
            if debug:
                self.dma("scalar", dbg_lg[T * 128:(T + 1) * 128, :], slg[:], reads=[slg_b])
                self.dma("scalar", dbg_m[T * 128:(T + 1) * 128, 0:12], srw[:, 16:28],
                         reads=[srw_b])
                self.dma("scalar", dbg_m[T * 128:(T + 1) * 128, 12:24], srw[:, 68:80],
                         reads=[srw_b])
            step(p, "vector", lambda v: v.scalar_tensor_tensor(out=swf[:], in0=srw[:, 32:44],
                                                               scalar=1.5, in1=srw[:, 16:28],
                                                               op0=ALU.mult, op1=ALU.mult),
                 reads=[srw_b], writes=[swf_b], inc=self.dve)
            step(p, "vector", lambda v: v.tensor_reduce(out=srw[:, 8:9], in_=swf[:, 8:12],
                                                        axis=AX.X, op=ALU.add),
                 reads=[swf_b], writes=[srw_b], inc=self.dve)
            step(p, "vector", lambda v, T=T: v.tensor_copy(out=szacc[:, T:T + 1],
                                                            in_=srw[:, 8:9]),
                 reads=[srw_b], writes=[szacc_b], inc=self.dve)
            pTt, pT_b = pT12
            step(p, "tensor", lambda t: t.transpose(pTt[:], swf[:], sidf[:]),
                 reads=[swf_b, sidf_b], writes=[pT_b], inc=self.pe)
            step(p, "vector", lambda v: v.tensor_copy(out=swft[:], in_=pTt[:]),
                 reads=[pT_b], writes=[swft_b], inc=self.dve)
            self.dma("vector", self.hbm["wfullT_h"][:, T * 128:(T + 1) * 128], swft[:],
                     reads=[swft_b])

        jobs = [(self.hbm[xin], self.hbm["ln_fm"], self.hbm["ln_tm"])]
        self.emit_ln(es, bar, "f0", jobs, tile_hook=router_hook)
        self.dma("scalar", self.hbm["z_tm"], szacc[:], reads=[szacc_b])
        es.close()

    def emit_F1(self):
        es = contextlib.ExitStack()
        bar = self.barrier_toks()
        p = self.p
        sones, sones_b = self.sb(es, "f1_ones", [1, 128], F32)
        self.dma("sync", sones[:], self.hbm["ones1"], writes=[sones_b], extra_wait=bar)
        swrow, swrow_b = self.sb(es, "f1_wrow", [1, L], F32)
        swtile, swtile_b = self.sb(es, "f1_wtile", [128, L], BF16)
        pW = self.ps(es, "f1_pW", [128, 512], F32)
        sgT, sgT_b = self.sb(es, "f1_gT", [128, KC * 1024], BF16)
        suT, suT_b = self.sb(es, "f1_uT", [128, KC * 1024], BF16)
        swst, swst_b = self.sb(es, "f1_wst", [128, 8 * 1024], F32)
        sln = [self.sb(es, f"f1_lnr{i}", [128, KC * 512], BF16) for i in range(2)]
        ssg, ssg_b = self.sb(es, "f1_sg", [128, 512], BF16)
        st1, st1_b = self.sb(es, "f1_t1", [128, 512], BF16)
        saw, saw_b = self.sb(es, "f1_aw", [128, 8 * 512], BF16)
        pG = self.ps(es, "f1_pG", [128, 512], F32)
        pU = self.ps(es, "f1_pU", [128, 512], F32)

        for e in range(8):
            pWt, pW_b = pW
            self.dma("sync", swrow[:], self.hbm["wfullT_h"][e:e + 1, :],
                     writes=[swrow_b], extra_wait=bar)
            for j in range(8):
                step(p, "tensor",
                     lambda t, j=j: t.matmul(pWt[:], sones[:], swrow[:, j * 512:(j + 1) * 512],
                                             start=True, stop=True),
                     reads=[sones_b, swrow_b], writes=[pW_b], inc=self.pe)
                step(p, "vector",
                     lambda v, j=j: v.tensor_copy(out=swtile[:, j * 512:(j + 1) * 512],
                                                  in_=pWt[:]),
                     reads=[pW_b], writes=[swtile_b], inc=self.dve)
            for w_hbm, dst, dst_b in ((self.hbm["gateT"], sgT, sgT_b),
                                      (self.hbm["upT"], suT, suT_b)):
                wsrc = w_hbm[e].rearrange("(kc p) i -> p kc i", p=128)
                for h2 in range(2):
                    self.dma("sync", swst[:].rearrange("p (kc i) -> p kc i", i=1024),
                             wsrc[:, h2 * 8:(h2 + 1) * 8, :], writes=[swst_b],
                             extra_wait=bar)
                    step(p, "vector",
                         lambda v, h2=h2, dst=dst: v.tensor_copy(
                             out=dst[:, h2 * 8 * 1024:(h2 + 1) * 8 * 1024], in_=swst[:]),
                         reads=[swst_b], writes=[dst_b], inc=self.dve)
            for grp in range(8):
                par = grp % 2
                lnb, lnb_b = sln[par]
                src = self.hbm["ln_fm"][:, :, grp * 512:(grp + 1) * 512]
                for h8 in range(2):
                    self.dma("sync",
                             lnb[:, h8 * 8 * 512:(h8 + 1) * 8 * 512].rearrange(
                                 "p (kc t) -> p kc t", t=512),
                             src[:, h8 * 8:(h8 + 1) * 8, :], writes=[lnb_b],
                             extra_wait=bar)
                for ic in range(8):
                    pGt, pG_b = pG
                    pUt, pU_b = pU
                    def gmm(t, ic=ic, lnb=lnb, wt=sgT, pp=pGt):
                        ins = None
                        for kc in range(KC):
                            ins = t.matmul(pp[:],
                                           wt[:, kc * 1024 + ic * 128:kc * 1024 + (ic + 1) * 128],
                                           lnb[:, kc * 512:(kc + 1) * 512],
                                           start=(kc == 0), stop=(kc == KC - 1))
                        return ins
                    step(p, "tensor", gmm, reads=[sgT_b, lnb_b], writes=[pG_b], inc=self.pe)
                    def umm(t, ic=ic, lnb=lnb):
                        ins = None
                        for kc in range(KC):
                            ins = t.matmul(pUt[:],
                                           suT[:, kc * 1024 + ic * 128:kc * 1024 + (ic + 1) * 128],
                                           lnb[:, kc * 512:(kc + 1) * 512],
                                           start=(kc == 0), stop=(kc == KC - 1))
                        return ins
                    step(p, "tensor", umm, reads=[suT_b, lnb_b], writes=[pU_b], inc=self.pe)
                    step(p, "scalar",
                         lambda s: s.activation(out=ssg[:], in_=pGt[:], func=AF.Silu,
                                                bias=0.0, scale=1.0),
                         reads=[pG_b], writes=[ssg_b], inc=self.act)
                    step(p, "vector",
                         lambda v: v.tensor_tensor(out=st1[:], in0=pUt[:], in1=ssg[:],
                                                   op=ALU.mult),
                         reads=[pU_b, ssg_b], writes=[st1_b], inc=self.dve)
                    step(p, "gpsimd",
                         lambda g, ic=ic, grp=grp: g.tensor_tensor(
                             out=saw[:, ic * 512:(ic + 1) * 512], in0=st1[:],
                             in1=swtile[:, grp * 512:(grp + 1) * 512], op=ALU.mult),
                         reads=[st1_b, swtile_b], writes=[saw_b], inc=self.gp)
                for tsub in range(4):
                    tile = grp * 4 + tsub
                    src3 = saw[:].rearrange("p (ic ts t) -> p ic ts t", ts=4, t=128)[
                        :, :, tsub, :]
                    dst3 = self.hbm["actw_t"][tile, :, e, :, :]
                    self.dma("scalar", dst3, src3, reads=[saw_b])
        es.close()

    def emit_F2(self, xin, xout):
        es = contextlib.ExitStack()
        bar = self.barrier_toks()
        p = self.p
        sdT, sdT_b = self.sb(es, "f2_dT", [128, 64 * 512], BF16)
        sdst, sdst_b = self.sb(es, "f2_dst", [128, 8 * 512], F32)
        szall, szall_b = self.sb(es, "f2_z", [128, NT], F32)
        self.dma("sync", szall[:], self.hbm["z_tm"], writes=[szall_b], extra_wait=bar)
        sat = [self.sb(es, f"f2_at{i}", [128, 64 * 128], BF16) for i in range(2)]
        slnr = [self.sb(es, f"f2_ln{i}", [128, 512], BF16) for i in range(2)]
        sxr = [self.sb(es, f"f2_x{i}", [128, 512], F32) for i in range(2)]
        so1, so1_b = self.sb(es, "f2_o1", [128, 512], F32)
        so2 = [self.sb(es, f"f2_o2{i}", [128, 512], F32) for i in range(2)]
        pD = [self.ps(es, f"f2_pD{i}", [128, 512], F32) for i in range(2)]

        for dg in range(4):
            for e in range(8):
                dsrc = self.hbm["downT"][e].rearrange("(ic p) do -> p ic do", p=128)[
                    :, :, dg * 512:(dg + 1) * 512]
                self.dma("sync", sdst[:].rearrange("p (ic do) -> p ic do", do=512), dsrc,
                         writes=[sdst_b], extra_wait=bar)
                step(p, "vector",
                     lambda v, e=e: v.tensor_copy(out=sdT[:, e * 4096:(e + 1) * 4096],
                                                  in_=sdst[:]),
                     reads=[sdst_b], writes=[sdT_b], inc=self.dve)
            for tile in range(NT):
                att, at_b = sat[tile % 2]
                self.dma("sync", att[:].rearrange("p (ec t) -> p ec t", t=128),
                         self.hbm["actw_t"][tile].rearrange("p e ic t -> p (e ic) t"),
                         writes=[at_b], extra_wait=bar)
                pp, pp_b = pD[tile % 2]
                def dmm(t, att=att, pp=pp):
                    ins = None
                    for ec in range(64):
                        ins = t.matmul(pp[:], att[:, ec * 128:(ec + 1) * 128],
                                       sdT[:, ec * 512:(ec + 1) * 512],
                                       start=(ec == 0), stop=(ec == 63))
                    return ins
                step(p, "tensor", dmm, reads=[at_b, sdT_b], writes=[pp_b], inc=self.pe)
                lnb, lnb_b = slnr[tile % 2]
                xb, xb_b = sxr[tile % 2]
                ob, ob_b = so2[tile % 2]
                self.dma("vector", lnb[:],
                         self.hbm["ln_tm"][tile * 128:(tile + 1) * 128,
                                           dg * 512:(dg + 1) * 512],
                         writes=[lnb_b], extra_wait=bar)
                self.dma("vector", xb[:],
                         self.hbm[xin][tile * 128:(tile + 1) * 128, dg * 512:(dg + 1) * 512],
                         writes=[xb_b], extra_wait=bar)
                step(p, "vector",
                     lambda v, tile=tile, lnb=lnb, pp=pp: v.scalar_tensor_tensor(
                         out=so1[:], in0=lnb[:], scalar=szall[:, tile:tile + 1],
                         in1=pp[:], op0=ALU.mult, op1=ALU.add),
                     reads=[lnb_b, pp_b, szall_b], writes=[so1_b], inc=self.dve)
                step(p, "vector",
                     lambda v, ob=ob, xb=xb: v.tensor_add(out=ob[:], in0=so1[:], in1=xb[:]),
                     reads=[so1_b, xb_b], writes=[ob_b], inc=self.dve)
                self.dma("scalar",
                         self.hbm[xout][tile * 128:(tile + 1) * 128, dg * 512:(dg + 1) * 512],
                         ob[:], reads=[ob_b])
        es.close()

    def emit_A0(self):
        es = contextlib.ExitStack()
        bar = self.barrier_toks()
        jobs = [(self.hbm["x_all"][b], self.hbm["ln_fm_all"][b], None) for b in range(8)]
        self.emit_ln(es, bar, "a0", jobs)
        es.close()

    def emit_B0(self):
        es = contextlib.ExitStack()
        bar = self.barrier_toks()
        self.emit_ln(es, bar, "b0", [(self.hbm["x2"], self.hbm["ln_fm"], None)])
        es.close()

    def emit_B1(self):
        es = contextlib.ExitStack()
        bar = self.barrier_toks()
        p = self.p
        pRT = self.ps(es, "b1_pRT", [128, 512], F32)
        sbqt, sbq_b = self.row_tile(es, bar, "b1_bq", self.hbm["bq"], D, out_dt=BF16, ps=pRT)
        sbkt, sbk_b = self.row_tile(es, bar, "b1_bk", self.hbm["bk"], D, out_dt=BF16, ps=pRT)
        sbvt, sbv_b = self.row_tile(es, bar, "b1_bv", self.hbm["bv"], D, out_dt=BF16, ps=pRT)
        sqnw, sqnw_b = self.row_tile(es, bar, "b1_qnw", self.hbm["qnw"], D, out_dt=BF16, ps=pRT)
        sknw, sknw_b = self.row_tile(es, bar, "b1_knw", self.hbm["knw"], D, out_dt=BF16, ps=pRT)
        smix, smix_b = self.row_tile(es, bar, "b1_mix", self.hbm["mix"], 5, ps=pRT)

        swst, swst_b = self.sb(es, "b1_wst", [128, 2 * D], F32)
        swT, swT_b = self.sb(es, "b1_wT", [128, KC * D], BF16)
        sln = [self.sb(es, f"b1_lnr{i}", [128, KC * 512], BF16) for i in range(2)]
        sqf, sqf_b = self.sb(es, "b1_qf", [128, D], F32)
        sqn, sqn_b = self.sb(es, "b1_qn", [128, D], BF16)
        sqm, sqm_b = self.sb(es, "b1_qm", [128, D], BF16)
        sjunk, sjunk_b = self.sb(es, "b1_junk", [128, D], BF16)
        sst, sst_b = self.sb(es, "b1_st", [128, 4], F32)
        pQ = self.ps(es, "b1_pQ", [128, D], F32)

        kinds = [("q", self.hbm["wqT"], sbqt, sbq_b, self.hbm["qmix_tm"]),
                 ("k", self.hbm["wkT"], sbkt, sbk_b, self.hbm["kn_tm"]),
                 ("v", self.hbm["wvT"], sbvt, sbv_b, self.hbm["v_tm"])]
        for kind, w_hbm, btile, btile_b, dst_hbm in kinds:
            wsrc = w_hbm.rearrange("(kc p) do -> p kc do", p=128)
            for q8 in range(8):
                self.dma("sync", swst[:].rearrange("p (kc do) -> p kc do", do=D),
                         wsrc[:, q8 * 2:(q8 + 1) * 2, :], writes=[swst_b], extra_wait=bar)
                step(p, "vector",
                     lambda v, q8=q8: v.tensor_copy(out=swT[:, q8 * 2 * D:(q8 + 1) * 2 * D],
                                                    in_=swst[:]),
                     reads=[swst_b], writes=[swT_b], inc=self.dve)
            for grp in range(8):
                par = grp % 2
                lnb, lnb_b = sln[par]
                src = self.hbm["ln_fm"][:, :, grp * 512:(grp + 1) * 512]
                for h8 in range(2):
                    self.dma("sync",
                             lnb[:, h8 * 8 * 512:(h8 + 1) * 8 * 512].rearrange(
                                 "p (kc t) -> p kc t", t=512),
                             src[:, h8 * 8:(h8 + 1) * 8, :], writes=[lnb_b],
                             extra_wait=bar)
                for ti in range(4):
                    tile = grp * 4 + ti
                    pQt, pQ_b = pQ
                    def pmm(t, ti=ti, lnb=lnb, pQt=pQt):
                        ins = None
                        for dg in range(4):
                            for kc in range(KC):
                                ins = t.matmul(
                                    pQt[:, dg * 512:(dg + 1) * 512],
                                    lnb[:, kc * 512 + ti * 128:kc * 512 + (ti + 1) * 128],
                                    swT[:, kc * D + dg * 512:kc * D + (dg + 1) * 512],
                                    start=(kc == 0), stop=(kc == KC - 1))
                        return ins
                    step(p, "tensor", pmm, reads=[lnb_b, swT_b], writes=[pQ_b], inc=self.pe)
                    if kind == "v":
                        step(p, "vector",
                             lambda v, pQt=pQt, btile=btile: v.scalar_tensor_tensor(
                                 out=sqn[:], in0=pQt[:], scalar=1.0, in1=btile[:],
                                 op0=ALU.mult, op1=ALU.add),
                             reads=[pQ_b, btile_b], writes=[sqn_b], inc=self.dve)
                        self.dma("scalar", dst_hbm[tile * 128:(tile + 1) * 128, :], sqn[:],
                                 reads=[sqn_b])
                        continue
                    step(p, "vector",
                         lambda v, pQt=pQt, btile=btile: v.tensor_add(out=sqf[:], in0=pQt[:],
                                                                      in1=btile[:]),
                         reads=[pQ_b, btile_b], writes=[sqf_b], inc=self.dve)
                    step(p, "vector",
                         lambda v: v.scalar_tensor_tensor(out=sjunk[:], in0=sqf[:], scalar=1.0,
                                                          in1=sqf[:], op0=ALU.mult,
                                                          op1=ALU.mult,
                                                          accum_out=sst[:, 0:1]),
                         reads=[sqf_b], writes=[sjunk_b, sst_b], inc=self.dve)
                    step(p, "vector",
                         lambda v: v.tensor_scalar(out=sst[:, 1:2], in0=sst[:, 0:1],
                                                   scalar1=1.0 / D, scalar2=1e-6,
                                                   op0=ALU.mult, op1=ALU.add),
                         reads=[sst_b], writes=[sst_b], inc=self.dve)
                    step(p, "scalar",
                         lambda s: s.activation(out=sst[:, 2:3], in_=sst[:, 1:2],
                                                func=AF.Sqrt, bias=0.0, scale=1.0),
                         reads=[sst_b], writes=[sst_b], inc=self.act)
                    step(p, "vector",
                         lambda v: v.reciprocal(out=sst[:, 3:4], in_=sst[:, 2:3]),
                         reads=[sst_b], writes=[sst_b], inc=self.dve)
                    gmt, gmt_b = (sqnw, sqnw_b) if kind == "q" else (sknw, sknw_b)
                    step(p, "vector",
                         lambda v, gmt=gmt: v.scalar_tensor_tensor(
                             out=sqn[:], in0=sqf[:], scalar=sst[:, 3:4], in1=gmt[:],
                             op0=ALU.mult, op1=ALU.mult),
                         reads=[sqf_b, sst_b, gmt_b], writes=[sqn_b], inc=self.dve)
                    if kind == "k":
                        self.dma("scalar", dst_hbm[tile * 128:(tile + 1) * 128, :], sqn[:],
                                 reads=[sqn_b])
                        continue
                    step(p, "vector",
                         lambda v: v.tensor_scalar(out=sqm[:], in0=sqn[:],
                                                   scalar1=smix[:, 0:1], scalar2=None,
                                                   op0=ALU.mult),
                         reads=[sqn_b, smix_b], writes=[sqm_b], inc=self.dve)
                    qn3 = sqn[:].rearrange("p (h d) -> p h d", d=128)
                    qm3 = sqm[:].rearrange("p (h d) -> p h d", d=128)
                    for n, c in enumerate(CH_SHIFTS):
                        step(p, "vector",
                             lambda v, n=n, c=c: v.scalar_tensor_tensor(
                                 out=qm3[:, :, c:128], in0=qn3[:, :, 0:128 - c],
                                 scalar=smix[:, n + 1:n + 2], in1=qm3[:, :, c:128],
                                 op0=ALU.mult, op1=ALU.add),
                             reads=[sqn_b, smix_b, sqm_b], writes=[sqm_b], inc=self.dve)
                        step(p, "vector",
                             lambda v, n=n, c=c: v.scalar_tensor_tensor(
                                 out=qm3[:, :, 0:c], in0=qn3[:, :, 128 - c:128],
                                 scalar=smix[:, n + 1:n + 2], in1=qm3[:, :, 0:c],
                                 op0=ALU.mult, op1=ALU.add),
                             reads=[sqn_b, smix_b, sqm_b], writes=[sqm_b], inc=self.dve)
                    self.dma("scalar", dst_hbm[tile * 128:(tile + 1) * 128, :], sqm[:],
                             reads=[sqm_b])
        es.close()

    def emit_B2(self):
        es = contextlib.ExitStack()
        bar = self.barrier_toks()
        p = self.p
        idt, idt_b = self.load_ident(es, bar)
        smask, smask_b = self.sb(es, "b2_mask", [128, 2048], BF16)
        self.dma("sync", smask[:], self.hbm["maskM"], writes=[smask_b], extra_wait=bar)
        EXT = 6848
        skext, skext_b = self.sb(es, "b2_kext", [128, 4 * EXT], BF16)
        sqfm, sqfm_b = self.sb(es, "b2_qfm", [128, 4 * L], BF16)
        skt, skt_b = self.sb(es, "b2_kt", [128, 512], BF16)
        sqt, sqt_b = self.sb(es, "b2_qt", [128, 512], BF16)
        svsh = [self.sb(es, f"b2_vsh{i}", [128, 16 * 512], BF16) for i in range(2)]
        stexp, stexp_b = self.sb(es, "b2_texp", [128, 2048], BF16)
        sme, sme_b = self.sb(es, "b2_me", [128, 2048], BF16)
        sden, sden_b = self.sb(es, "b2_den", [128, 4], F32)
        srec, srec_b = self.sb(es, "b2_rec", [128, 4], F32)
        so4, so4_b = self.sb(es, "b2_o4", [128, 512], BF16)
        pST = [self.ps(es, f"b2_pST{i}", [128, 1024], F32) for i in range(2)]
        pO4 = [self.ps(es, f"b2_pO4{i}", [128, 512], F32) for i in range(2)]
        ppTb = self.ps(es, "b2_ppTb", [128, 512], BF16)

        for hg in range(4):
            for srcname, dstt, dst_b, base, width in (
                    ("kn_tm", skext, skext_b, 2048, EXT),
                    ("qmix_tm", sqfm, sqfm_b, 0, L)):
                stg, stg_b = (skt, skt_b) if srcname == "kn_tm" else (sqt, sqt_b)
                for T in range(NT):
                    self.dma("sync", stg[:],
                             self.hbm[srcname][T * 128:(T + 1) * 128,
                                               hg * 512:(hg + 1) * 512],
                             writes=[stg_b], extra_wait=bar)
                    pt, pt_b = ppTb
                    def ktr(t, stg=stg, pt=pt):
                        ins = None
                        for h in range(4):
                            ins = t.transpose(pt[:, h * 128:(h + 1) * 128],
                                              stg[:, h * 128:(h + 1) * 128], idt[:])
                        return ins
                    step(p, "tensor", ktr, reads=[stg_b, idt_b], writes=[pt_b], inc=self.pe)
                    dview = dstt[:].rearrange("p (h c) -> p h c", c=width)[
                        :, :, base + T * 128:base + (T + 1) * 128]
                    step(p, "vector",
                         lambda v, dview=dview, pt=pt: v.tensor_copy(
                             out=dview, in_=pt[:].rearrange("p (h c) -> p h c", c=128)),
                         reads=[pt_b], writes=[dst_b], inc=self.dve)
            kv = skext[:].rearrange("p (h c) -> p h c", c=EXT)
            for h in range(4):
                self.dma("sync", kv[:, h, 0:2048], kv[:, h, 4096:6144],
                         reads=[skext_b], writes=[skext_b], extra_wait=bar)
                self.dma("sync", kv[:, h, 6144:6848], kv[:, h, 2048:2752],
                         reads=[skext_b], writes=[skext_b], extra_wait=bar)

            for T in range(NT):
                vb, vb_b = svsh[T % 2]
                l0 = T * 128
                QQ = ("sync", "gpsimd")
                for i, sft in enumerate(SEQ_SHIFTS):
                    r0 = (l0 - sft) % L
                    n1 = min(128, L - r0)
                    self.dma(QQ[i % 2], vb[:n1, i * 512:(i + 1) * 512],
                             self.hbm["v_tm"][r0:r0 + n1, hg * 512:(hg + 1) * 512],
                             writes=[vb_b], extra_wait=bar)
                    if n1 < 128:
                        self.dma(QQ[(i + 1) % 2], vb[n1:128, i * 512:(i + 1) * 512],
                                 self.hbm["v_tm"][0:128 - n1, hg * 512:(hg + 1) * 512],
                                 writes=[vb_b], extra_wait=bar)
                po, po_b = pO4[T % 2]
                for h in range(4):
                    for hf in range(2):
                        ps_, ps_b = pST[hf]
                        def stmm(t, h=h, hf=hf, l0=l0, ps_=ps_):
                            ins = None
                            for j in range(8):
                                sft = SEQ_SHIFTS[hf * 8 + j]
                                col = h * EXT + 2048 + l0 - sft
                                ins = t.matmul(ps_[:, j * 128:(j + 1) * 128],
                                               skext[:, col:col + 128],
                                               sqfm[:, h * L + l0:h * L + l0 + 128],
                                               start=True, stop=True)
                            return ins
                        step(p, "tensor", stmm, reads=[skext_b, sqfm_b], writes=[ps_b],
                             inc=self.pe)
                        step(p, "scalar",
                             lambda s, hf=hf, ps_=ps_: s.activation(
                                 out=stexp[:, hf * 1024:(hf + 1) * 1024], in_=ps_[:],
                                 func=AF.Exp, bias=0.0, scale=INV_SQRT_HD),
                             reads=[ps_b], writes=[stexp_b], inc=self.act)
                    for hf in range(2):
                        step(p, "vector",
                             lambda v, hf=hf: v.scalar_tensor_tensor(
                                 out=sme[:, hf * 1024:(hf + 1) * 1024],
                                 in0=stexp[:, hf * 1024:(hf + 1) * 1024], scalar=1.0,
                                 in1=smask[:, hf * 1024:(hf + 1) * 1024],
                                 op0=ALU.mult, op1=ALU.mult,
                                 accum_out=sden[:, hf:hf + 1]),
                             reads=[stexp_b, smask_b], writes=[sme_b, sden_b],
                             inc=self.dve)
                    step(p, "vector",
                         lambda v: v.tensor_add(out=sden[:, 2:3], in0=sden[:, 0:1],
                                                in1=sden[:, 1:2]),
                         reads=[sden_b], writes=[sden_b], inc=self.dve)
                    step(p, "vector",
                         lambda v, h=h: v.reciprocal(out=srec[:, h:h + 1],
                                                     in_=sden[:, 2:3]),
                         reads=[sden_b], writes=[srec_b], inc=self.dve)
                    def vmm(t, h=h, po=po, vb=vb):
                        ins = None
                        for i in range(16):
                            ins = t.matmul(po[:, h * 128:(h + 1) * 128],
                                           sme[:, i * 128:(i + 1) * 128],
                                           vb[:, i * 512 + h * 128:i * 512 + (h + 1) * 128],
                                           start=(i == 0), stop=(i == 15))
                        return ins
                    step(p, "tensor", vmm, reads=[sme_b, vb_b], writes=[po_b], inc=self.pe)
                for h in range(4):
                    step(p, "vector",
                         lambda v, h=h, po=po: v.tensor_scalar(
                             out=so4[:, h * 128:(h + 1) * 128],
                             in0=po[:, h * 128:(h + 1) * 128],
                             scalar1=srec[:, h:h + 1], scalar2=None, op0=ALU.mult),
                         reads=[po_b, srec_b], writes=[so4_b], inc=self.dve)
                self.dma("scalar",
                         self.hbm["out4_tm"][l0:l0 + 128, hg * 512:(hg + 1) * 512],
                         so4[:], reads=[so4_b])
        es.close()

    def emit_B3(self):
        es = contextlib.ExitStack()
        bar = self.barrier_toks()
        p = self.p
        idt, idt_b = self.load_ident(es, bar)
        pRT = self.ps(es, "b3_pRT", [128, 512], F32)
        sbot, sbo_b = self.row_tile(es, bar, "b3_bo", self.hbm["bo"], D, ps=pRT)
        swst, swst_b = self.sb(es, "b3_wst", [128, 4 * D], F32)
        swo, swo_b = self.sb(es, "b3_wo", [128, KC * D], BF16)
        wsrc = self.hbm["woT"].rearrange("(hc p) do -> p hc do", p=128)
        for q4 in range(4):
            self.dma("sync", swst[:].rearrange("p (hc do) -> p hc do", do=D),
                     wsrc[:, q4 * 4:(q4 + 1) * 4, :], writes=[swst_b], extra_wait=bar)
            step(p, "vector",
                 lambda v, q4=q4: v.tensor_copy(out=swo[:, q4 * 4 * D:(q4 + 1) * 4 * D],
                                                in_=swst[:]),
                 reads=[swst_b], writes=[swo_b], inc=self.dve)
        so4t = [self.sb(es, f"b3_o4t{i}", [128, D], BF16) for i in range(2)]
        sofm = [self.sb(es, f"b3_ofm{i}", [128, D], BF16) for i in range(2)]
        sxr = [self.sb(es, f"b3_x{i}", [128, 512], F32) for i in range(2)]
        so1, so1_b = self.sb(es, "b3_o1", [128, 512], F32)
        so2 = [self.sb(es, f"b3_o2{i}", [128, 512], F32) for i in range(2)]
        ptr = [self.ps(es, f"b3_ptr{i}", [128, D], BF16) for i in range(2)]
        pO = [self.ps(es, f"b3_pO{i}", [128, 512], F32) for i in range(2)]

        for tile in range(NT):
            par = tile % 2
            ot, ot_b = so4t[par]
            fm, fm_b = sofm[par]
            pt, pt_b = ptr[par]
            self.dma("sync", ot[:],
                     self.hbm["out4_tm"][tile * 128:(tile + 1) * 128, :],
                     writes=[ot_b], extra_wait=bar)
            def otr(t, ot=ot, pt=pt):
                ins = None
                for hc in range(KC):
                    ins = t.transpose(pt[:, hc * 128:(hc + 1) * 128],
                                      ot[:, hc * 128:(hc + 1) * 128], idt[:])
                return ins
            step(p, "tensor", otr, reads=[ot_b, idt_b], writes=[pt_b], inc=self.pe)
            step(p, "vector",
                 lambda v, fm=fm, pt=pt: v.tensor_copy(out=fm[:], in_=pt[:]),
                 reads=[pt_b], writes=[fm_b], inc=self.dve)
            for dg in range(4):
                pp, pp_b = pO[dg % 2]
                xb, xb_b = sxr[dg % 2]
                ob, ob_b = so2[dg % 2]
                def omm(t, dg=dg, fm=fm, pp=pp):
                    ins = None
                    for hc in range(KC):
                        ins = t.matmul(pp[:], fm[:, hc * 128:(hc + 1) * 128],
                                       swo[:, hc * D + dg * 512:hc * D + (dg + 1) * 512],
                                       start=(hc == 0), stop=(hc == KC - 1))
                    return ins
                step(p, "tensor", omm, reads=[fm_b, swo_b], writes=[pp_b], inc=self.pe)
                self.dma("vector", xb[:],
                         self.hbm["x2"][tile * 128:(tile + 1) * 128, dg * 512:(dg + 1) * 512],
                         writes=[xb_b], extra_wait=bar)
                step(p, "vector",
                     lambda v, dg=dg, pp=pp: v.scalar_tensor_tensor(
                         out=so1[:], in0=pp[:], scalar=1.0,
                         in1=sbot[:, dg * 512:(dg + 1) * 512],
                         op0=ALU.mult, op1=ALU.add),
                     reads=[pp_b, sbo_b], writes=[so1_b], inc=self.dve)
                step(p, "vector",
                     lambda v, ob=ob, xb=xb: v.tensor_add(out=ob[:], in0=so1[:], in1=xb[:]),
                     reads=[so1_b, xb_b], writes=[ob_b], inc=self.dve)
                self.dma("scalar",
                         self.hbm["x3"][tile * 128:(tile + 1) * 128, dg * 512:(dg + 1) * 512],
                         ob[:], reads=[ob_b])
        es.close()


def np_bf16(x):
    return np.asarray(x, dtype=ml_dtypes.bfloat16)


def make_inputs(core, inp):
    wqkvT = np.ascontiguousarray(inp["L0_attn_a_Wqkv"].T)
    sel = np.ascontiguousarray(
        wqkvT.reshape(D, 16, 384)[:, [core, core + 8], :].reshape(D, 768))
    maskM = np.zeros((128, 2048), np.float32)
    for p in range(128):
        maskM[p, p::128] = 1.0
    d = {
        "x_all": inp["L0_attn_a_query"],
        "x_own": np.ascontiguousarray(inp["L0_attn_a_query"][core]),
        "wqkvt_sel": sel,
        "wprojT": np.ascontiguousarray(inp["L0_attn_a_Wproj"].T),
        "bproj": inp["L0_attn_a_bproj"].reshape(1, D),
        "clsT": np.ascontiguousarray(inp["L0_ffn_classifier_w"].T),
        "corr": inp["L0_ffn_e_score_correction_bias"].reshape(1, 12),
        "gateT": np.ascontiguousarray(np.transpose(inp["L0_ffn_gate_w"], (0, 2, 1))),
        "upT": np.ascontiguousarray(np.transpose(inp["L0_ffn_up_w"], (0, 2, 1))),
        "downT": np.ascontiguousarray(np.transpose(inp["L0_ffn_down_w"], (0, 2, 1))),
        "wqT": np.ascontiguousarray(inp["L0_attn_b_wq"].T),
        "wkT": np.ascontiguousarray(inp["L0_attn_b_wk"].T),
        "wvT": np.ascontiguousarray(inp["L0_attn_b_wv"].T),
        "woT": np.ascontiguousarray(inp["L0_attn_b_wo"].T),
        "bq": inp["L0_attn_b_bq"].reshape(1, D),
        "bk": inp["L0_attn_b_bk"].reshape(1, D),
        "bv": inp["L0_attn_b_bv"].reshape(1, D),
        "bo": inp["L0_attn_b_bo"].reshape(1, D),
        "qnw": inp["L0_attn_b_q_norm_w"].reshape(1, D),
        "knw": inp["L0_attn_b_k_norm_w"].reshape(1, D),
        "mix": inp["L0_attn_b_score_mix_w"].reshape(1, 5),
        "ident_bf": np_bf16(np.eye(128, dtype=np.float32)),
        "ident_f": np.eye(128, dtype=np.float32),
        "iota12": np.broadcast_to(np.arange(12, dtype=np.float32), (128, 12)).copy(),
        "maskM": np_bf16(maskM),
        "ones1": np.ones((1, 128), np.float32),
    }
    return d


EMITTERS = {}


def build(phases):
    nc = bass.Bass("TRN2", target_bir_lowering=False)
    B = Bld(nc, phases)
    for ph in phases:
        EMITTERS[ph](B)
    end = B.barrier_toks()
    B.p.do("sync", lambda e: e.nop(), wait=end)
    B.p.realize(B.es)
    B.es.close()
    return nc, B


EMITTERS["A0"] = lambda B: B.emit_A0()
EMITTERS["A1"] = lambda B: B.emit_A1()
EMITTERS["A2"] = lambda B: B.emit_A2()
import os
EMITTERS["F0a"] = lambda B: B.emit_F0("x1", debug=bool(os.environ.get("MK_DEBUG")))
EMITTERS["F1a"] = lambda B: B.emit_F1()
EMITTERS["F2a"] = lambda B: B.emit_F2("x1", "x2")
EMITTERS["B0"] = lambda B: B.emit_B0()
EMITTERS["B1"] = lambda B: B.emit_B1()
EMITTERS["B2"] = lambda B: B.emit_B2()
EMITTERS["B3"] = lambda B: B.emit_B3()
EMITTERS["F0b"] = lambda B: B.emit_F0("x3")
EMITTERS["F1b"] = lambda B: B.emit_F1()
EMITTERS["F2b"] = lambda B: B.emit_F2("x3", "x4")




from concourse.bass_utils import run_bass_kernel_spmd


def kernel(**inputs):
    nc, B = build(ALL_PHASES)
    in_maps = []
    for core in range(8):
        m = {k: v for k, v in make_inputs(core, inputs).items() if k in B.used_inputs}
        in_maps.append(m)
    res = run_bass_kernel_spmd(nc, in_maps, core_ids=list(range(8)))
    out = np.stack([np.asarray(res.results[c]["x4"], np.float32) for c in range(8)], axis=0)
    return out
